# version 38
# baseline (speedup 1.0000x reference)
import math
import numpy as np
import concourse.bass as bass
import concourse.mybir as mybir
from concourse.bass_utils import run_bass_kernel_spmd
from contextlib import ExitStack

F32 = mybir.dt.float32
BF16 = mybir.dt.bfloat16
AF = mybir.ActivationFunctionType
ALU = mybir.AluOpType
AXX = mybir.AxisListType.X

D = 2048
NTOK = 2048
NCH = 16
HEADS = 16
T = 512
NT = 4
SCALE = 128.0 ** -0.5
TWO_PI = 2.0 * math.pi
C_INVF = -math.log(10000.0) / 64.0

DEV_PHASES = None
DEV_TAP_FN = None
DEV_C_STOP = None


class SemW:
    __slots__ = ("h", "v")

    def __init__(self, h):
        self.h = h
        self.v = 0


class Sched:
    ENGINES = ("tensor", "vector", "scalar", "gpsimd", "sync")
    SERIAL = {"vector", "scalar", "gpsimd"}

    def __init__(self, serial=None):
        self.ops = {e: [] for e in self.ENGINES}
        self.serial = serial

    def emit(self, eng, fn, inc=None, inc_by=1, is_dma=False, chain=True):
        chained = (self.serial is not None and eng in self.SERIAL
                   and eng in self.serial["sems"])
        if chained:
            prev = self.serial["last"].get(eng)
            if prev is not None and chain:
                self.wait(eng, prev[0], prev[1])
            if inc is None and not is_dma:
                inc = self.serial["sems"][eng]
                inc_by = 1

        def op(e, fn=fn, inc=inc, inc_by=inc_by):
            ins = fn(e)
            if inc is not None:
                ins.then_inc(inc.h, inc_by)
        self.ops[eng].append(op)
        if inc is not None:
            inc.v += inc_by
            if chained and not is_dma:
                self.serial["last"][eng] = (inc, inc.v)
            return inc.v
        return None

    def wait(self, eng, sem, val):
        if val and val > 0:
            self.ops[eng].append(lambda e, s=sem, v=val: e.wait_ge(s.h, v))

    def barrier_in(self, engines, sems):
        for eng in engines:
            for s in sems:
                self.wait(eng, s, s.v)

    def flush(self, block):
        for name in self.ENGINES:
            lst = self.ops[name]
            if not lst:
                continue
            getattr(block, name)(lambda e, lst=lst: [f(e) for f in lst])
        self.ops = {e: [] for e in self.ENGINES}


def build_nc():
    nc = bass.Bass("TRN2", target_bir_lowering=False, debug=False)

    xT = nc.dram_tensor("xT", [D, NTOK], F32, kind="ExternalInput").ap()
    wq = nc.dram_tensor("wq", [NCH, 128, D], F32, kind="ExternalInput").ap()
    wkv = nc.dram_tensor("wkv", [2, 128, D], F32, kind="ExternalInput").ap()
    wout = nc.dram_tensor("wout", [NCH, 128, D], F32, kind="ExternalInput").ap()
    wffn = nc.dram_tensor("wffn", [NCH, 128, D], F32, kind="ExternalInput").ap()
    scale_in = nc.dram_tensor("scale_t", [128, NCH], F32, kind="ExternalInput").ap()
    cosk_in = nc.dram_tensor("cosk_t", [128, NTOK], F32, kind="ExternalInput").ap()
    sink_in = nc.dram_tensor("sink_t", [128, NTOK], F32, kind="ExternalInput").ap()
    out = nc.dram_tensor("out", [D, NTOK], F32, kind="ExternalOutput").ap()

    phases = DEV_PHASES if DEV_PHASES is not None else ["A", "B0", "C0", "D0", "B1", "C1", "D1"]

    with ExitStack() as es:
        def sb(name, shape, dtype, side=None):
            kw = {} if side is None else {"side": side}
            return es.enter_context(nc.sbuf_tensor(name, shape, dtype, **kw))

        def mksem(name):
            return SemW(es.enter_context(nc.semaphore(name)))

        PE = mksem("PE"); SC = mksem("SC"); VE = mksem("VE")
        GP = mksem("GP"); DO = mksem("DO")
        DWp = [mksem("DW0"), mksem("DW1")]
        DXI = mksem("DXI")
        DT = mksem("DT")
        DXC = mksem("DXC")
        DXC1 = mksem("DXC1")
        DXCp = [DXC, DXC1]
        BSEMS = (PE, SC, VE, GP, DWp[0], DWp[1], DXI, DT, DXC, DXC1, DO)
        BSEMS_B = (PE, SC, VE, GP, DWp[0], DWp[1], DXI, DT, DXC, DXC1)
        SERIAL = {"sems": {"vector": VE, "scalar": SC, "gpsimd": GP}, "last": {}}

        ones_sb = sb("ones_sb", [128, 128], BF16)
        ident = sb("ident", [128, 128], BF16)
        kT = sb("kT", [128, NTOK], BF16)
        v_sb = sb("v_sb", [128, NTOK], BF16)
        negM = sb("negM", [128, HEADS], F32)
        maxq2 = sb("maxq2", [128, HEADS], F32)
        kmax2 = sb("kmax2", [128, 1], F32)
        scale_sb = sb("scale_sb", [128, NCH], F32)
        eps_sb = sb("eps_sb", [128, 1], F32)
        mqp = sb("mqp", [128, NT], F32)
        pT = [sb(f"pT{i}", [128, 2 * T], BF16) for i in range(2)]
        rz = [sb(f"rz{i}", [128, T], F32) for i in range(2)]

        esq = [ExitStack(), ExitStack()]
        q_h = [None, None]
        q_h[1] = esq[1].enter_context(
            nc.sbuf_tensor("q_h1", [128, NCH * 1024], BF16, side="right"))
        q_h[0] = esq[0].enter_context(
            nc.sbuf_tensor("q_h0", [128, NCH * 1024], BF16, side="right"))

        state = {
            "git": 0,
            "chunk": 0,
            "pe_z": {},
            "ve_norm": {},
            "pe_col": {0: 0, 1: 0},
            "wpref": {},
            "xpref": {},
            "sc_x1b": {0: 0, 1: 0},
            "dapply": [],
        }

        with nc.Block() as block:
            with ExitStack() as esa:
                def sba(name, shape, dtype):
                    return esa.enter_context(nc.sbuf_tensor(name, shape, dtype))
                xb = sba("xb", [128, NCH * NTOK], BF16)
                cosk = sba("cosk", [128, NTOK], F32)
                sink = sba("sink", [128, NTOK], F32)
                coskb = sba("coskb", [128, NTOK], BF16)
                sinkb = sba("sinkb", [128, NTOK], BF16)
                qscr = [sba(f"qscr{i}", [128, T], BF16) for i in range(2)]
                qrot = [sba(f"qrot{i}", [128, T], BF16) for i in range(2)]
                srotb = sba("srotb", [128, NTOK], BF16)
                wcol_a = [sba(f"wcol_a{i}", [128, D], BF16) for i in range(2)]
                sq_scr = [sba(f"sq_scr{i}", [128, T], BF16) for i in range(2)]
                redb = [sba(f"redb{i}", [128, T], BF16) for i in range(2)]
                vtscr = [sba(f"vtscr{i}", [128, T], BF16) for i in range(2)]

                s = Sched(SERIAL)
                s.emit("vector", lambda e: e.memset(ident[:], 1.0), inc=VE)
                s.emit("vector", lambda e: e.memset(eps_sb[:], 1e-5), inc=VE)
                ve_id = VE.v
                s.emit("gpsimd", lambda e: e.memset(ones_sb[:], 1.0), inc=GP)
                s.wait("gpsimd", VE, ve_id)
                s.emit("gpsimd", lambda e: e.affine_select(
                    ident[:], ident[:], [[1, 128]], ALU.is_equal, 0.0,
                    base=0, channel_multiplier=-1), inc=GP)
                gp_consts = GP.v
                s.wait("sync", DT, DT.v)
                s.emit("sync", lambda e: e.dma_start(out=scale_sb[:], in_=scale_in),
                       inc=DT, inc_by=16)
                for tab_sb, tab_in in ((cosk, cosk_in), (sink, sink_in)):
                    s.wait("sync", DT, DT.v)
                    s.emit("sync", lambda e, tab_sb=tab_sb, tab_in=tab_in: e.dma_start(
                        out=tab_sb[:], in_=tab_in), inc=DT, inc_by=16)
                dt_tabs = DT.v
                s.wait("vector", DT, dt_tabs)
                s.emit("vector", lambda e: e.tensor_copy(coskb[:], cosk[:]))
                s.emit("vector", lambda e: e.tensor_copy(sinkb[:], sink[:]), inc=VE)
                s.emit("vector", lambda e: e.tensor_scalar(
                    out=srotb[0:64, :], in0=sinkb[0:64, :], scalar1=-1.0,
                    scalar2=None, op0=ALU.mult))
                s.emit("vector", lambda e: e.tensor_copy(
                    srotb[64:128, :], sinkb[64:128, :]), inc=VE)
                ve_tabs_b16 = VE.v

                s.emit("gpsimd", lambda e: e.dma_start(
                    out=xb[:].rearrange("p (kc t) -> p kc t", kc=NCH),
                    in_=xT.rearrange("(kc p) t -> p kc t", p=128)), inc=DXI,
                    inc_by=16, is_dma=True)
                dxi_all = DXI.v

                rope_a, rope_b = redb[0], redb[1]

                def rope_evac(ps_ap, dst_lo_fn, ctb, stb_rot, qs, qr):
                    s.emit("scalar", lambda e: e.activation(qs[:], ps_ap[:], AF.Copy),
                           inc=SC)
                    sc_cp = SC.v
                    s.wait("vector", SC, sc_cp)
                    s.emit("vector", lambda e: e.tensor_copy(qr[0:64, :], qs[64:128, :]))
                    s.emit("vector", lambda e: e.tensor_copy(qr[64:128, :], qs[0:64, :]))
                    s.emit("vector", lambda e: e.tensor_mul(rope_a[:], qs[:], ctb))
                    s.emit("vector", lambda e: e.tensor_mul(rope_b[:], qr[:], stb_rot))
                    ve = s.emit("vector", lambda e: e.tensor_add(
                        dst_lo_fn(0, 128), rope_a[:], rope_b[:]), inc=VE)
                    return sc_cp, ve

                with nc.psum_tensor("qp0", [128, T], F32) as qp0, \
                     nc.psum_tensor("qp1", [128, T], F32) as qp1, \
                     nc.psum_tensor("qp2", [128, T], F32) as qp2, \
                     nc.psum_tensor("qp3", [128, T], F32) as qp3, \
                     nc.psum_tensor("qn_pr0", [128, T], F32) as qn_pr0, \
                     nc.psum_tensor("qn_pr1", [128, T], F32) as qn_pr1, \
                     nc.psum_tensor("tp_ps", [128, 128], BF16) as tp_ps:
                    qp = [qp0, qp1, qp2, qp3]
                    qn_pr = [qn_pr0, qn_pr1]
                    ve_qred = {}
                    pe_col_last = {0: 0, 1: 0}
                    pe_qn_last = {0: 0, 1: 0}
                    ve_rope = {}

                    def col_load(src_ap, m, buf):
                        s.wait("gpsimd", PE, pe_col_last[buf])
                        s.emit("gpsimd", lambda e, src_ap=src_ap, m=m, buf=buf: e.dma_start(
                            out=wcol_a[buf][:], in_=src_ap[m]), inc=DWp[buf], inc_by=16,
                            is_dma=True)
                        return (buf, DWp[buf].v)

                    ve_maxred = {}
                    sc_qcopy = {}
                    sc_qfree = {}
                    sc_kvfree = {}
                    pend = []

                    def flush_pend(n_keep=0):
                        while len(pend) > n_keep:
                            pend.pop(0)()

                    def q_tile(m, tt, dw_need):
                        idx = m * NT + tt
                        b = idx % 4
                        lo, hi = tt * T, (tt + 1) * T
                        flush_pend(1)
                        s.wait("tensor", DWp[dw_need[0]], dw_need[1])
                        s.wait("tensor", DXI, dxi_all)
                        if idx >= 4:
                            pidx = idx - 4
                            s.wait("tensor", SC, sc_qfree[(pidx // NT, pidx % NT)])
                        for k in range(NCH):
                            s.emit("tensor", lambda e, b=b, m=m, k=k, lo=lo, hi=hi: e.matmul(
                                qp[b][:], wcol_a[m % 2][:, k * 128:(k + 1) * 128],
                                xb[:, k * NTOK + lo: k * NTOK + hi],
                                start=(k == 0), stop=(k == NCH - 1)),
                                inc=(PE if k == NCH - 1 else None))
                        pe_col_last[m % 2] = PE.v
                        pe_q = PE.v
                        qdst = q_h[0] if tt < 2 else q_h[1]
                        cbase = m * 1024 + (tt % 2) * T
                        s.wait("scalar", PE, pe_q)
                        if m == 0 and tt == 0:
                            s.wait("vector", VE, ve_tabs_b16)
                        if idx >= 2:
                            pidx = idx - 2
                            s.wait("scalar", VE, ve_rope[(pidx // NT, pidx % NT)])

                        def dstf(plo, phi, qdst=qdst, cbase=cbase):
                            return qdst[plo:phi, cbase:cbase + T]
                        sc_qcopy[(m, tt)], ve_rope[(m, tt)] = rope_evac(
                            qp[b], dstf, coskb[:, lo:hi], srotb[:, lo:hi],
                            qscr[idx % 2], qrot[idx % 2])
                        sqb = (m * NT + tt) % 2
                        s.wait("scalar", PE, pe_qn_last[sqb])
                        s.emit("scalar", lambda e, b=b, sqb=sqb: e.activation(
                            sq_scr[sqb][:], qp[b][:], AF.Square), inc=SC,
                            chain=False)
                        sc_sq = SC.v
                        sc_qfree[(m, tt)] = sc_sq

                        def qn_mm(m=m, tt=tt, idx=idx, sqb=sqb, sc_sq=sc_sq):
                            nb = idx % 2
                            s.wait("tensor", SC, sc_sq)
                            if idx >= 2:
                                s.wait("tensor", VE, ve_qred[idx - 2])
                            s.emit("tensor", lambda e, nb=nb: e.matmul(
                                qn_pr[nb][:], ones_sb[:], sq_scr[sqb][:],
                                start=True, stop=True), inc=PE)
                            pe_qn_last[sqb] = PE.v
                            s.wait("vector", PE, PE.v)
                            ve_qred[idx] = s.emit(
                                "vector", lambda e, nb=nb, tt=tt: e.tensor_reduce(
                                    mqp[:, tt:tt + 1], qn_pr[nb][:], axis=AXX,
                                    op=ALU.max), inc=VE)
                            if tt == NT - 1:
                                ve_maxred[m] = s.emit(
                                    "vector", lambda e, m=m: e.tensor_reduce(
                                        maxq2[:, m:m + 1], mqp[:], axis=AXX,
                                        op=ALU.max), inc=VE)
                        pend.append(qn_mm)

                    for m in range(NCH):
                        dw_c = col_load(wq, m, m % 2)
                        for tt in range(NT):
                            q_tile(m, tt, dw_c)
                    flush_pend(0)

                    kv_state = {"pe_kn": 0, "pe_tp_last": {}}
                    ve_kv = {}
                    sc_kvcopy = {}

                    def kv_tile(mkv, tt, dw_need):
                        idx = mkv * NT + tt
                        b = (64 + idx) % 4
                        lo, hi = tt * T, (tt + 1) * T
                        flush_pend(1)
                        s.wait("tensor", DWp[dw_need[0]], dw_need[1])
                        if idx >= 4:
                            pidx = idx - 4
                            pk = (pidx // NT, pidx % NT)
                            if pk in sc_kvfree:
                                s.wait("tensor", SC, sc_kvfree[pk])
                            else:
                                s.wait("tensor", VE, ve_kv.get(pk))
                        else:
                            s.wait("tensor", SC, sc_qfree[(NCH - 1, idx)])
                        for k in range(NCH):
                            s.emit("tensor", lambda e, b=b, mkv=mkv, k=k, lo=lo, hi=hi: e.matmul(
                                qp[b][:], wcol_a[mkv % 2][:, k * 128:(k + 1) * 128],
                                xb[:, k * NTOK + lo: k * NTOK + hi],
                                start=(k == 0), stop=(k == NCH - 1)),
                                inc=(PE if k == NCH - 1 else None))
                        pe_col_last[mkv % 2] = PE.v
                        pe_kv = PE.v
                        if mkv == 0:
                            s.wait("scalar", PE, pe_kv)
                            if idx >= 2:
                                pk = (idx - 2 // NT, (idx - 2) % NT)
                                pk = ((idx - 2) // NT, (idx - 2) % NT)
                                if pk in ve_kv and pk in sc_kvcopy:
                                    s.wait("scalar", VE, ve_kv[pk])
                            else:
                                s.wait("scalar", VE, ve_rope[(NCH - 1, 2 + idx)])

                            def dstk(plo, phi, lo=lo, hi=hi):
                                return kT[plo:phi, lo:hi]
                            sc_kvcopy[(mkv, tt)], ve_kv[(mkv, tt)] = rope_evac(
                                qp[b], dstk, coskb[:, lo:hi], srotb[:, lo:hi],
                                qscr[idx % 2], qrot[idx % 2])
                            sqb = tt % 2
                            s.wait("scalar", PE, pe_qn_last[sqb])
                            s.emit("scalar", lambda e, b=b, sqb=sqb: e.activation(
                                sq_scr[sqb][:], qp[b][:], AF.Square), inc=SC,
                                chain=False)
                            sck = SC.v
                            sc_kvfree[(mkv, tt)] = sck

                            def kn_mm(tt=tt, sqb=sqb, sck=sck):
                                gidx = 64 + tt
                                nb = gidx % 2
                                s.wait("tensor", SC, sck)
                                s.wait("tensor", VE, ve_qred[gidx - 2])
                                s.emit("tensor", lambda e, nb=nb: e.matmul(
                                    qn_pr[nb][:], ones_sb[:], sq_scr[sqb][:],
                                    start=True, stop=True), inc=PE)
                                kv_state["pe_kn"] = PE.v
                                pe_qn_last[sqb] = PE.v
                                s.wait("vector", PE, PE.v)
                                ve_qred[gidx] = s.emit(
                                    "vector", lambda e, nb=nb, tt=tt: e.tensor_reduce(
                                        mqp[:, tt:tt + 1], qn_pr[nb][:], axis=AXX,
                                        op=ALU.max), inc=VE)
                            pend.append(kn_mm)
                        else:
                            s.wait("vector", PE, pe_kv)
                            if tt >= 2:
                                s.wait("vector", PE, kv_state["pe_tp_last"].get(tt - 2, 0))
                            ve_kv[(mkv, tt)] = s.emit("vector", lambda e, b=b, tt=tt: e.tensor_copy(
                                vtscr[tt % 2][:], qp[b][:]), inc=VE)
                            ve_tp_prev = None
                            for j in range(4):
                                jj = tt * 4 + j
                                s.wait("tensor", VE, ve_kv[(mkv, tt)])
                                if ve_tp_prev is not None:
                                    s.wait("tensor", VE, ve_tp_prev)
                                elif kv_state["pe_tp_last"]:
                                    pass
                                s.emit("tensor", lambda e, tt=tt, j=j: e.transpose(
                                    tp_ps[:], vtscr[tt % 2][:, j * 128:(j + 1) * 128],
                                    ident[:]), inc=PE)
                                pe_tp = PE.v
                                s.wait("vector", PE, pe_tp)
                                ve_tp_prev = s.emit("vector", lambda e, jj=jj: e.tensor_copy(
                                    v_sb[:, jj * 128:(jj + 1) * 128], tp_ps[:]), inc=VE)
                            kv_state["pe_tp_last"][tt] = pe_tp

                    for mkv in range(2):
                        dw_c = col_load(wkv, mkv, mkv % 2)
                        for tt in range(NT):
                            kv_tile(mkv, tt, dw_c)
                    flush_pend(0)

                    s.emit("vector", lambda e: e.tensor_reduce(
                        kmax2[:], mqp[:], axis=AXX, op=ALU.max), inc=VE)
                    ve_kmax = VE.v

                    s.wait("vector", VE, ve_kmax)
                    s.emit("vector", lambda e: e.tensor_scalar(
                        out=negM[:], in0=maxq2[:], scalar1=kmax2[:], scalar2=None,
                        op0=ALU.mult), inc=VE)
                    s.wait("scalar", VE, VE.v)
                    s.emit("scalar", lambda e: e.activation(negM[:], negM[:], AF.Sqrt), inc=SC)
                    s.wait("vector", SC, SC.v)
                    s.emit("vector", lambda e: e.tensor_scalar(
                        out=negM[:], in0=negM[:], scalar1=-SCALE, scalar2=None,
                        op0=ALU.mult), inc=VE)

                    s.flush(block)
                    if DEV_TAP_FN is not None:
                        DEV_TAP_FN("A", dict(
                            nc=nc, block=block, Sched=Sched, sems=dict(
                                PE=PE, SC=SC, VE=VE, GP=GP, DO=DO),
                            kT=kT, v_sb=v_sb, negM=negM, maxq2=maxq2, kmax2=kmax2,
                            q_h=q_h, xb=xb, cosk=cosk,
                            sink=sink, scale_sb=scale_sb))

            with ExitStack() as escd:
                def sbc(name, shape, dtype):
                    return escd.enter_context(nc.sbuf_tensor(name, shape, dtype))
                wcol_c = [sbc(f"wcol_c{i}", [128, D], BF16) for i in range(2)]
                xcol = sbc("xcol", [128, 1024], F32)
                cast_w = [sbc(f"cast_w{i}", [128, 2 * T], BF16) for i in range(2)]
                sq_w = [sbc(f"sq_w{i}", [128, 2 * T], BF16) for i in range(2)]
                mean_w = sbc("mean_w", [128, 2 * T], F32)
                rstd_w = sbc("rstd_w", [128, 2 * T], F32)
                wg1 = xcol
                xcol_b = [xcol, rstd_w]
                g_w = mean_w


                def prefetch_cols(sch, src_ap, tag, h, ms):
                    for m in ms:
                        buf = m % 2
                        sch.wait("gpsimd", PE, state["pe_col"][buf])
                        sch.emit("gpsimd", lambda e, m=m, buf=buf, src_ap=src_ap:
                                 e.dma_start(out=wcol_c[buf][:], in_=src_ap[m]),
                                 inc=DWp[buf], inc_by=16, is_dma=True)
                        state["wpref"][(tag, h, m)] = (buf, DWp[buf].v)

                def phase_B(s, h, o_ps, z_ps, s_ps, oTf):
                    def one_head(t, hh):
                        g = state["git"]
                        ob = 0
                        Kt = 4 * t + 4
                        P = Kt // 2
                        base = state["chunk"]
                        qcol0 = hh * 1024 + (t % 2) * T
                        qsrc = q_h[h]
                        pe_s = {}
                        sc_exp = {}
                        gp_mask = {}

                        def emit_S(kc):
                            lo = max(0, kc - 4 * t) * 128
                            p, hf = kc // 2, kc % 2
                            s.emit("tensor", lambda e, kc=kc, lo=lo, p=p, hf=hf:
                                   e.matmul(s_ps[p % 3][:, hf * T + lo: hf * T + T],
                                            kT[:, kc * 128:(kc + 1) * 128],
                                            qsrc[:, qcol0 + lo: qcol0 + T],
                                            start=True, stop=True), inc=PE)
                            pe_s[kc] = PE.v

                        def emit_AV(kc):
                            lo = max(0, kc - 4 * t) * 128
                            n = base + kc
                            jp, hf = (n // 2) % 2, kc % 2
                            if kc >= 4 * t:
                                s.wait("tensor", GP, gp_mask[kc])
                            else:
                                s.wait("tensor", SC, sc_exp[kc])
                            if kc == 0:
                                s.wait("tensor", VE, state["ve_norm"].get(g - 1, 0))
                            s.emit("tensor", lambda e, kc=kc, lo=lo, jp=jp, hf=hf, ob=ob, Kt=Kt:
                                   e.matmul(o_ps[ob][:, lo:T],
                                            v_sb[:, kc * 128:(kc + 1) * 128],
                                            pT[jp][:, hf * T + lo: hf * T + T],
                                            start=(kc == 0), stop=(kc == Kt - 1)), inc=PE)

                        def emit_Z(kc):
                            lo = max(0, kc - 4 * t) * 128
                            n = base + kc
                            jp, hf = (n // 2) % 2, kc % 2
                            s.emit("tensor", lambda e, kc=kc, lo=lo, jp=jp, hf=hf, ob=ob, Kt=Kt:
                                   e.matmul(z_ps[ob][:, lo:T], ones_sb[:],
                                            pT[jp][:, hf * T + lo: hf * T + T],
                                            start=(kc == 0), stop=(kc == Kt - 1)), inc=PE)
                            state["pe_z"][n] = PE.v

                        def emit_exp(p):
                            n0 = base + 2 * p
                            jp = (n0 // 2) % 2
                            s.wait("scalar", PE, pe_s[2 * p + 1])
                            s.wait("scalar", PE, state["pe_z"].get(n0 - 3, 0))
                            if 2 * p >= 4 * t:
                                for hf in range(2):
                                    kc = 2 * p + hf
                                    lo = max(0, kc - 4 * t) * 128
                                    s.emit("scalar", lambda e, jp=jp, hf=hf, lo=lo, p=p, hh=hh:
                                           e.activation(pT[jp][:, hf * T + lo: hf * T + T],
                                                        s_ps[p % 3][:, hf * T + lo: hf * T + T],
                                                        AF.Exp, bias=negM[:, hh:hh + 1],
                                                        scale=SCALE), inc=SC, chain=False)
                                    sc_exp[kc] = SC.v
                            else:
                                s.emit("scalar", lambda e, jp=jp, p=p, hh=hh:
                                       e.activation(pT[jp][:, :],
                                                    s_ps[p % 3][:, :],
                                                    AF.Exp, bias=negM[:, hh:hh + 1],
                                                    scale=SCALE), inc=SC, chain=False)
                                sc_exp[2 * p] = sc_exp[2 * p + 1] = SC.v
                            for hf in range(2):
                                kc = 2 * p + hf
                                if kc >= 4 * t:
                                    lo = (kc - 4 * t) * 128
                                    s.wait("gpsimd", SC, sc_exp[kc])
                                    s.emit("gpsimd", lambda e, jp=jp, hf=hf, lo=lo:
                                           e.affine_select(
                                               pT[jp][:, hf * T + lo: hf * T + lo + 128],
                                               pT[jp][:, hf * T + lo: hf * T + lo + 128],
                                               [[1, 128]], ALU.is_ge, 0.0,
                                               base=0, channel_multiplier=-1), inc=GP,
                                           chain=False)
                                    gp_mask[kc] = GP.v

                        def emit_AVZ_pair(p):
                            emit_AV(2 * p)
                            emit_AV(2 * p + 1)
                            emit_Z(2 * p)
                            emit_Z(2 * p + 1)

                        for p in range(P):
                            emit_S(2 * p)
                            emit_S(2 * p + 1)
                            if p >= 2:
                                emit_AVZ_pair(p - 2)
                            if p >= 1:
                                emit_exp(p - 1)
                        emit_exp(P - 1)
                        emit_AVZ_pair(P - 2)
                        emit_AVZ_pair(P - 1)
                        state["chunk"] += Kt

                        ocol = hh * 1024 + (t % 2) * T
                        s.wait("scalar", PE, state["pe_z"][base + Kt - 1])
                        s.emit("scalar", lambda e, ob=ob: e.activation(
                            rz[ob][:], z_ps[ob][:], AF.Ln), chain=False)
                        sc_rz = s.emit("scalar", lambda e, ob=ob: e.activation(
                            rz[ob][:], rz[ob][:], AF.Exp, scale=-1.0), inc=SC)
                        s.wait("vector", SC, sc_rz)
                        state["ve_norm"][g] = s.emit(
                            "vector", lambda e, ob=ob, ocol=ocol: e.tensor_mul(
                                oTf[:, ocol:ocol + T], o_ps[ob][:], rz[ob][:]), inc=VE,
                            chain=False)
                        state["git"] += 1
                        if state["dapply"]:
                            state["dapply"].pop(0)(s)

                    for t in (2 * h, 2 * h + 1):
                        for hh in range(HEADS):
                            one_head(t, hh)
                    while state["dapply"]:
                        state["dapply"].pop(0)(s)

                def phase_C(s, h, y_ps, mn_ps, sq_ps, oTf, s_h, x1b_h):
                    pe_y = {}
                    ve_evac = {}
                    pe_stats = {}

                    def one_m(m):
                        buf = m % 2
                        if ("C", h, m) in state["wpref"]:
                            dw_c = state["wpref"].pop(("C", h, m))
                        else:
                            s.wait("gpsimd", PE, state["pe_col"][buf])
                            s.emit("gpsimd", lambda e, m=m, buf=buf: e.dma_start(
                                out=wcol_c[buf][:], in_=wout[m]), inc=DWp[buf], inc_by=16,
                                is_dma=True)
                            dw_c = (buf, DWp[buf].v)
                        xb2 = m % 2
                        if m == 0 and h in state["xpref"]:
                            dx_x = state["xpref"].pop(h)
                        else:
                            if m > 1:
                                s.wait("sync", VE, ve_evac[(m - 2, 1)])
                            s.emit("sync", lambda e, m=m, h=h, xb2=xb2: e.dma_start(
                                out=xcol_b[xb2][:],
                                in_=xT[m * 128:(m + 1) * 128, h * 1024:(h + 1) * 1024]),
                                inc=DXCp[xb2], inc_by=16)
                            dx_x = DXCp[xb2].v
                        for tt in range(2):
                            b2 = (2 * m + tt) % 2
                            idx = 2 * m + tt
                            flush_pend(1)
                            s.wait("tensor", DWp[dw_c[0]], dw_c[1])
                            if idx >= 2:
                                s.wait("tensor", VE, ve_evac[((idx - 2) // 2, (idx - 2) % 2)])
                            for k in range(NCH):
                                s.emit("tensor", lambda e, b2=b2, m=m, k=k, tt=tt: e.matmul(
                                    y_ps[b2][:], wcol_c[m % 2][:, k * 128:(k + 1) * 128],
                                    oTf[:, k * 1024 + tt * T: k * 1024 + (tt + 1) * T],
                                    start=(k == 0), stop=(k == NCH - 1)),
                                    inc=(PE if k == NCH - 1 else None))
                            pe_y[(m, tt)] = PE.v
                            state["pe_col"][m % 2] = PE.v
                            scol = m * 1024 + tt * T
                            s.wait("vector", PE, pe_y[(m, tt)])
                            s.wait("vector", DXCp[xb2], dx_x)
                            ve_evac[(m, tt)] = s.emit(
                                "vector", lambda e, b2=b2, scol=scol, tt=tt, xb2=xb2:
                                e.tensor_add(
                                    s_h[:, scol:scol + T], y_ps[b2][:],
                                    xcol_b[xb2][:, tt * T:(tt + 1) * T]), inc=VE,
                                chain=False)
                            s.wait("scalar", VE, ve_evac[(m, tt)])
                            if idx >= 4:
                                s.wait("scalar", PE, pe_stats[((idx - 4) // 2, (idx - 4) % 2)])
                            wb, wt = (idx // 2) % 2, idx % 2
                            s.emit("scalar", lambda e, wb=wb, wt=wt, scol=scol: e.activation(
                                cast_w[wb][:, wt * T:(wt + 1) * T],
                                s_h[:, scol:scol + T], AF.Copy), chain=False)
                            s.emit("scalar", lambda e, wb=wb, wt=wt, scol=scol: e.activation(
                                sq_w[wb][:, wt * T:(wt + 1) * T],
                                s_h[:, scol:scol + T], AF.Square), inc=SC, chain=False)
                            sc_c = SC.v

                            def st_mm(m=m, tt=tt, wb=wb, wt=wt, sc_c=sc_c):
                                s.wait("tensor", SC, sc_c)
                                s.emit("tensor", lambda e, tt=tt: e.matmul(
                                    mn_ps[:, tt * T:(tt + 1) * T], ones_sb[:],
                                    cast_w[wb][:, wt * T:(wt + 1) * T],
                                    start=(m == 0), stop=(m == NCH - 1)), inc=PE)
                                s.emit("tensor", lambda e, tt=tt: e.matmul(
                                    sq_ps[:, tt * T:(tt + 1) * T], ones_sb[:],
                                    sq_w[wb][:, wt * T:(wt + 1) * T],
                                    start=(m == 0), stop=(m == NCH - 1)), inc=PE)
                                pe_stats[(m, tt)] = PE.v
                            pend.append(st_mm)

                    pend = []

                    def flush_pend(n_keep=0):
                        while len(pend) > n_keep:
                            pend.pop(0)()

                    for m in range(NCH):
                        one_m(m)
                    flush_pend(0)
                    prefetch_cols(s, wffn, "D", h, [0, 1])

                    if DEV_C_STOP == "mm":
                        return
                    ve_rstd = ln_stats(s, mn_ps[:], sq_ps[:], mean_w[:],
                                       rstd_w[:], pe_stats[(NCH - 1, 1)], wg1)
                    if DEV_C_STOP == "stats":
                        return

                    sc_x1b = {}
                    for tt in range(2):
                        for m in range(NCH):
                            scol = m * 1024 + tt * T
                            s.wait("vector", ve_rstd[0], ve_rstd[1])
                            s.emit("vector", lambda e, scol=scol, tt=tt: e.tensor_sub(
                                s_h[:, scol:scol + T], s_h[:, scol:scol + T],
                                mean_w[:, tt * T:(tt + 1) * T]), chain=False)
                            vex = s.emit("vector", lambda e, scol=scol, tt=tt: e.tensor_mul(
                                s_h[:, scol:scol + T], s_h[:, scol:scol + T],
                                rstd_w[:, tt * T:(tt + 1) * T]), inc=VE)
                            s.wait("scalar", VE, vex)
                            s.emit("scalar", lambda e, scol=scol: e.activation(
                                x1b_h[:, scol:scol + T], s_h[:, scol:scol + T],
                                AF.Copy), inc=SC, chain=False)
                        sc_x1b[tt] = SC.v
                    state["sc_x1b"] = sc_x1b
                    state["c_ve"] = VE.v
                    state["c_gp"] = GP.v

                def ln_stats(s, m_ps, q_ps, m_sb, r_sb, pe_done, scr):
                    s.wait("vector", PE, pe_done)
                    s.emit("vector", lambda e: e.tensor_scalar(
                        out=m_sb, in0=m_ps, scalar1=1.0 / D, scalar2=None,
                        op0=ALU.mult))
                    s.emit("vector", lambda e: e.scalar_tensor_tensor(
                        out=r_sb, in0=m_ps, scalar=1.0 / D, in1=m_sb,
                        op0=ALU.mult, op1=ALU.mult))
                    s.emit("vector", lambda e: e.scalar_tensor_tensor(
                        out=scr[:], in0=q_ps, scalar=1.0 / D, in1=r_sb,
                        op0=ALU.mult, op1=ALU.subtract), inc=VE)
                    s.wait("scalar", VE, VE.v)
                    s.emit("scalar", lambda e: e.activation(
                        r_sb, scr[:], AF.Ln, bias=eps_sb[:, 0:1]))
                    scv = s.emit("scalar", lambda e: e.activation(
                        r_sb, r_sb, AF.Exp, scale=-0.5), inc=SC)
                    return (SC, scv)

                def phase_D(s, h, f_ps, fsq_ps, mn_ps, sq_ps, f_h, s_h, x1b_h,
                            defer_apply=False):
                    pe_f = {}
                    sc_fe = {}
                    pe_fsq = {}

                    def load_col(m):
                        buf = m % 2
                        if ("D", h, m) in state["wpref"]:
                            return state["wpref"].pop(("D", h, m))
                        s.wait("gpsimd", PE, state["pe_col"][buf])
                        s.emit("gpsimd", lambda e, m=m, buf=buf: e.dma_start(
                            out=wcol_c[buf][:], in_=wffn[m]), inc=DWp[buf], inc_by=16,
                            is_dma=True)
                        return (buf, DWp[buf].v)

                    def one_mt(pi, m, tt, dw_c):
                        b2 = pi % 2
                        flush_pend(1)
                        s.wait("tensor", DWp[dw_c[0]], dw_c[1])
                        s.wait("tensor", SC, state["sc_x1b"][tt])
                        if pi >= 2:
                            s.wait("tensor", SC, sc_fe[pi - 2])
                        for k in range(NCH):
                            s.emit("tensor", lambda e, b2=b2, m=m, k=k, tt=tt: e.matmul(
                                f_ps[b2][:], wcol_c[m % 2][:, k * 128:(k + 1) * 128],
                                x1b_h[:, k * 1024 + tt * T: k * 1024 + (tt + 1) * T],
                                start=(k == 0), stop=(k == NCH - 1)),
                                inc=(PE if k == NCH - 1 else None))
                        pe_f[pi] = PE.v
                        state["pe_col"][m % 2] = PE.v
                        fcol = m * 1024 + tt * T
                        s.wait("scalar", PE, pe_f[pi])
                        if pi >= 4:
                            s.wait("scalar", PE, pe_fsq[pi - 4])
                        wb, wt = (pi // 2) % 2, pi % 2
                        s.emit("scalar", lambda e, b2=b2, fcol=fcol, m=m: e.activation(
                            f_h[:, fcol:fcol + T], f_ps[b2][:], AF.Copy,
                            scale=scale_sb[:, m:m + 1]), chain=False)
                        s.emit("scalar", lambda e, wb=wb, wt=wt, fcol=fcol: e.activation(
                            sq_w[wb][:, wt * T:(wt + 1) * T],
                            f_h[:, fcol:fcol + T], AF.Square), inc=SC)
                        sc_fe[pi] = SC.v

                        def fsq_mm(pi=pi, m=m, tt=tt, wb=wb, wt=wt, sc=sc_fe[pi]):
                            s.wait("tensor", SC, sc)
                            s.emit("tensor", lambda e, tt=tt: e.matmul(
                                fsq_ps[:, tt * T:(tt + 1) * T], ones_sb[:],
                                sq_w[wb][:, wt * T:(wt + 1) * T],
                                start=(m == 0), stop=(m == NCH - 1)), inc=PE)
                            pe_fsq[pi] = PE.v
                        pend.append(fsq_mm)

                    pend = []

                    def flush_pend(n_keep=0):
                        while len(pend) > n_keep:
                            pend.pop(0)()

                    pi = 0
                    for mp in range(0, NCH, 2):
                        dw_a = load_col(mp)
                        dw_b = load_col(mp + 1)
                        for (m, tt, dw_c) in ((mp, 0, dw_a), (mp + 1, 0, dw_b),
                                              (mp, 1, dw_a), (mp + 1, 1, dw_b)):
                            one_mt(pi, m, tt, dw_c)
                            pi += 1
                    flush_pend(0)

                    s.wait("scalar", PE, pe_fsq[2 * NCH - 1])
                    s.wait("scalar", GP, state["c_gp"])
                    s.wait("vector", GP, state["c_gp"])
                    s.emit("scalar", lambda e: e.activation(
                        wg1[:], fsq_ps[:], AF.Sqrt), inc=SC)
                    s.wait("vector", SC, SC.v)
                    s.emit("vector", lambda e: e.tensor_scalar(
                        out=wg1[:], in0=wg1[:], scalar1=1e-7, scalar2=None,
                        op0=ALU.max), inc=VE)
                    s.wait("scalar", VE, VE.v)
                    s.emit("scalar", lambda e: e.activation(
                        rstd_w[:], wg1[:], AF.Tanh), inc=SC)
                    s.emit("vector", lambda e: e.tensor_scalar(
                        out=g_w[:], in0=wg1[:], scalar1=1e-7, scalar2=None,
                        op0=ALU.add), inc=VE)
                    s.wait("scalar", VE, VE.v)
                    s.emit("scalar", lambda e: e.activation(g_w[:], g_w[:], AF.Ln))
                    s.emit("scalar", lambda e: e.activation(
                        g_w[:], g_w[:], AF.Exp, scale=-1.0), inc=SC)
                    s.wait("vector", SC, SC.v)
                    s.emit("vector", lambda e: e.tensor_mul(g_w[:], g_w[:], rstd_w[:]))
                    s.emit("vector", lambda e: e.tensor_mul(rstd_w[:], wg1[:], g_w[:]))
                    s.emit("vector", lambda e: e.tensor_scalar(
                        out=rstd_w[:], in0=rstd_w[:], scalar1=1e-7, scalar2=None,
                        op0=ALU.max), inc=VE)
                    s.wait("scalar", VE, VE.v)
                    s.emit("scalar", lambda e: e.activation(rstd_w[:], rstd_w[:], AF.Ln))
                    s.emit("scalar", lambda e: e.activation(
                        rstd_w[:], rstd_w[:], AF.Exp, scale=-1.0), inc=SC)
                    s.wait("vector", SC, SC.v)
                    s.emit("vector", lambda e: e.tensor_scalar(
                        out=rstd_w[:], in0=rstd_w[:], scalar1=0.99, scalar2=1.0,
                        op0=ALU.mult, op1=ALU.min))
                    ve_gate = s.emit("vector", lambda e: e.tensor_mul(
                        g_w[:], g_w[:], rstd_w[:]), inc=VE)

                    pe_stats2 = {}
                    for m in range(NCH):
                        b2 = m % 2
                        wlo = m * 1024
                        s.emit("vector", lambda e, wlo=wlo: e.tensor_mul(
                            wg1[:], f_h[:, wlo:wlo + 1024], g_w[:]))
                        ve_s2 = s.emit("vector", lambda e, wlo=wlo: e.tensor_add(
                            s_h[:, wlo:wlo + 1024], s_h[:, wlo:wlo + 1024],
                            wg1[:]), inc=VE)
                        s.wait("scalar", VE, ve_s2)
                        if m >= 2:
                            s.wait("scalar", PE, pe_stats2[m - 2])
                        s.emit("scalar", lambda e, b2=b2, wlo=wlo: e.activation(
                            cast_w[b2][:], s_h[:, wlo:wlo + 1024], AF.Copy), chain=False)
                        sc_cw = s.emit("scalar", lambda e, b2=b2, wlo=wlo: e.activation(
                            sq_w[b2][:], s_h[:, wlo:wlo + 1024], AF.Square), inc=SC,
                            chain=False)
                        s.wait("tensor", SC, sc_cw)
                        for tt in range(2):
                            s.emit("tensor", lambda e, b2=b2, tt=tt, m=m: e.matmul(
                                mn_ps[:, tt * T:(tt + 1) * T], ones_sb[:],
                                cast_w[b2][:, tt * T:(tt + 1) * T],
                                start=(m == 0), stop=(m == NCH - 1)))
                            s.emit("tensor", lambda e, b2=b2, tt=tt, m=m: e.matmul(
                                sq_ps[:, tt * T:(tt + 1) * T], ones_sb[:],
                                sq_w[b2][:, tt * T:(tt + 1) * T],
                                start=(m == 0), stop=(m == NCH - 1)),
                                inc=(PE if tt == 1 else None))
                        pe_stats2[m] = PE.v

                    ve_rstd2 = ln_stats(s, mn_ps[:], sq_ps[:], mean_w[:],
                                        rstd_w[:], pe_stats2[NCH - 1], wg1)

                    def emit_apply(sb, m):
                        wlo = m * 1024
                        sb.wait("vector", ve_rstd2[0], ve_rstd2[1])
                        sb.emit("vector", lambda e, wlo=wlo: e.tensor_sub(
                            s_h[:, wlo:wlo + 1024], s_h[:, wlo:wlo + 1024],
                            mean_w[:]), chain=False)
                        vex = sb.emit("vector", lambda e, wlo=wlo: e.tensor_mul(
                            s_h[:, wlo:wlo + 1024], s_h[:, wlo:wlo + 1024],
                            rstd_w[:]), inc=VE)
                        sb.wait("sync", VE, vex)
                        sb.emit("sync", lambda e, m=m, h=h, wlo=wlo: e.dma_start(
                            out=out[m * 128:(m + 1) * 128,
                                    h * 1024:(h + 1) * 1024],
                            in_=s_h[:, wlo:wlo + 1024]), inc=DO, inc_by=16)

                    if defer_apply:
                        for m in range(NCH):
                            state["dapply"].append(
                                lambda sb, m=m: emit_apply(sb, m))
                    else:
                        for m in range(NCH):
                            emit_apply(s, m)

                ess = [ExitStack(), ExitStack()]
                for h in range(2):
                    with ExitStack() as esh:
                        oTf = esh.enter_context(
                            nc.sbuf_tensor(f"oTf{h}", [128, NCH * 1024], BF16))
                        f_h = oTf

                        if f"B{h}" in phases:
                            s = Sched(SERIAL)
                            s.barrier_in(("tensor", "scalar", "vector", "gpsimd", "sync"), BSEMS_B)
                            with nc.psum_tensor(f"o_ps{h}a", [128, T], F32) as opa, \
                                 nc.psum_tensor(f"z_ps{h}a", [128, T], F32) as zpa, \
                                 nc.psum_tensor(f"s_ps{h}a", [128, 2 * T], F32) as spa, \
                                 nc.psum_tensor(f"s_ps{h}b", [128, 2 * T], F32) as spb, \
                                 nc.psum_tensor(f"s_ps{h}c", [128, 2 * T], F32) as spc:
                                phase_B(s, h, [opa], [zpa], [spa, spb, spc], oTf)
                                prefetch_cols(s, wout, "C", h, [0, 1])
                                s.emit("sync", lambda e, h=h: e.dma_start(
                                    out=xcol[:],
                                    in_=xT[0:128, h * 1024:(h + 1) * 1024]),
                                    inc=DXC, inc_by=16)
                                state["xpref"][h] = DXC.v
                                s.flush(block)
                            if DEV_TAP_FN is not None:
                                DEV_TAP_FN(f"B{h}", dict(
                                    nc=nc, block=block, Sched=Sched, sems=dict(
                                        PE=PE, SC=SC, VE=VE, GP=GP, DO=DO),
                                    oTf=oTf))

                        if h == 1:
                            ess[0].close()
                        esq[h].close()
                        s_h = ess[h].enter_context(
                            nc.sbuf_tensor(f"s_h{h}", [128, NCH * 1024], F32,
                                           side="right"))
                        x1b_h = esh.enter_context(
                            nc.sbuf_tensor(f"x1b{h}", [128, NCH * 1024], BF16))

                        if f"C{h}" in phases:
                            s = Sched(SERIAL)
                            s.barrier_in(("tensor", "scalar", "vector", "gpsimd", "sync"), BSEMS)
                            with nc.psum_tensor(f"y_ps{h}a", [128, T], F32) as ypa, \
                                 nc.psum_tensor(f"y_ps{h}b", [128, T], F32) as ypb, \
                                 nc.psum_tensor(f"mn_ps{h}", [128, 2 * T], F32) as mpw, \
                                 nc.psum_tensor(f"sx_ps{h}", [128, 2 * T], F32) as sxw:
                                phase_C(s, h, [ypa, ypb], mpw, sxw,
                                        oTf, s_h, x1b_h)
                                s.flush(block)
                            if DEV_TAP_FN is not None:
                                DEV_TAP_FN(f"C{h}", dict(
                                    nc=nc, block=block, Sched=Sched, sems=dict(
                                        PE=PE, SC=SC, VE=VE, GP=GP, DO=DO),
                                    oTf=oTf, s_h=s_h, x1b_h=x1b_h))

                        if f"D{h}" in phases:
                            s = Sched(SERIAL)
                            with nc.psum_tensor(f"f_ps{h}a", [128, T], F32) as fpa, \
                                 nc.psum_tensor(f"f_ps{h}b", [128, T], F32) as fpb, \
                                 nc.psum_tensor(f"fq_ps{h}", [128, 2 * T], F32) as fqw, \
                                 nc.psum_tensor(f"m2_ps{h}", [128, 2 * T], F32) as m2w, \
                                 nc.psum_tensor(f"q2_ps{h}", [128, 2 * T], F32) as q2w:
                                phase_D(s, h, [fpa, fpb], fqw,
                                        m2w, q2w, f_h, s_h, x1b_h,
                                        defer_apply=(h == 0 and "B1" in phases))
                                s.flush(block)

                ess[1].close()
                s = Sched(SERIAL)
                s.wait("sync", DO, DO.v)
                s.flush(block)

    return nc


def _host_tables():
    try:
        import jax
        import jax.numpy as jnp
        cpu = jax.devices("cpu")[0]
        with jax.default_device(cpu):
            e = jnp.arange(0, 128, 2, dtype=jnp.float32) / 128.0
            inv = 1.0 / 10000.0 ** e
            freqs = jnp.arange(NTOK, dtype=jnp.float32)[:, None] * inv[None, :]
            pos = jnp.concatenate((freqs, freqs), axis=-1)
            cos = np.asarray(jnp.cos(pos))
            sin = np.asarray(jnp.sin(pos))
    except Exception:
        inv = (1.0 / 10000.0 ** (np.arange(0, 128, 2, dtype=np.float64) / 128.0)).astype(np.float32)
        f = (np.arange(NTOK, dtype=np.float32)[:, None] * inv[None, :]).astype(np.float32)
        pos = np.concatenate([f, f], axis=-1)
        cos = np.cos(pos).astype(np.float32)
        sin = np.sin(pos).astype(np.float32)
    return np.ascontiguousarray(cos.T), np.ascontiguousarray(sin.T)


def _re_w(W):
    W = np.asarray(W, np.float32)
    ncols = W.shape[1] // 128
    return np.ascontiguousarray(
        W.reshape(16, 128, ncols, 128).transpose(2, 1, 0, 3)
    ).reshape(ncols, 128, 2048)


def kernel(L0_attn_x, L0_attn_Wq, L0_attn_Wkv, L0_attn_Wout, L0_ffn_weight, L0_ffn_scale):
    x = np.asarray(L0_attn_x, np.float32)
    B = x.shape[0]
    WQ = _re_w(L0_attn_Wq)
    WKV = _re_w(L0_attn_Wkv)
    WOUT = _re_w(L0_attn_Wout)
    WFFN = _re_w(L0_ffn_weight)
    scale_t = np.ascontiguousarray(np.asarray(L0_ffn_scale, np.float32).reshape(16, 128).T)
    cosk_t, sink_t = _host_tables()

    nc = build_nc()
    in_maps = []
    for b in range(B):
        in_maps.append({
            "xT": np.ascontiguousarray(x[b].T),
            "wq": WQ, "wkv": WKV, "wout": WOUT, "wffn": WFFN,
            "scale_t": scale_t, "cosk_t": cosk_t, "sink_t": sink_t,
        })
    res = run_bass_kernel_spmd(nc, in_maps, core_ids=list(range(B)))
    return np.stack([np.ascontiguousarray(np.asarray(res.results[b]["out"]).T)
                     for b in range(B)])


# version 39
# speedup vs baseline: 1.0251x; 1.0251x over previous
import math
import numpy as np
import concourse.bass as bass
import concourse.mybir as mybir
from concourse.bass_utils import run_bass_kernel_spmd
from contextlib import ExitStack

F32 = mybir.dt.float32
BF16 = mybir.dt.bfloat16
AF = mybir.ActivationFunctionType
ALU = mybir.AluOpType
AXX = mybir.AxisListType.X

D = 2048
NTOK = 2048
NCH = 16
HEADS = 16
T = 512
NT = 4
SCALE = 128.0 ** -0.5
TWO_PI = 2.0 * math.pi
C_INVF = -math.log(10000.0) / 64.0

DEV_PHASES = None
DEV_TAP_FN = None
DEV_C_STOP = None


class SemW:
    __slots__ = ("h", "v")

    def __init__(self, h):
        self.h = h
        self.v = 0


class Sched:
    ENGINES = ("tensor", "vector", "scalar", "gpsimd", "sync")
    SERIAL = {"vector", "scalar", "gpsimd"}

    def __init__(self, serial=None):
        self.ops = {e: [] for e in self.ENGINES}
        self.serial = serial

    def emit(self, eng, fn, inc=None, inc_by=1, is_dma=False, chain=True):
        chained = (self.serial is not None and eng in self.SERIAL
                   and eng in self.serial["sems"])
        if chained:
            prev = self.serial["last"].get(eng)
            if prev is not None and chain:
                self.wait(eng, prev[0], prev[1])
            if inc is None and not is_dma:
                inc = self.serial["sems"][eng]
                inc_by = 1

        def op(e, fn=fn, inc=inc, inc_by=inc_by):
            ins = fn(e)
            if inc is not None:
                ins.then_inc(inc.h, inc_by)
        self.ops[eng].append(op)
        if inc is not None:
            inc.v += inc_by
            if chained and not is_dma:
                self.serial["last"][eng] = (inc, inc.v)
            return inc.v
        return None

    def wait(self, eng, sem, val):
        if val and val > 0:
            self.ops[eng].append(lambda e, s=sem, v=val: e.wait_ge(s.h, v))

    def barrier_in(self, engines, sems):
        for eng in engines:
            for s in sems:
                self.wait(eng, s, s.v)

    def flush(self, block):
        for name in self.ENGINES:
            lst = self.ops[name]
            if not lst:
                continue
            getattr(block, name)(lambda e, lst=lst: [f(e) for f in lst])
        self.ops = {e: [] for e in self.ENGINES}


def build_nc():
    nc = bass.Bass("TRN2", target_bir_lowering=False, debug=False)

    xT = nc.dram_tensor("xT", [D, NTOK], F32, kind="ExternalInput").ap()
    wq = nc.dram_tensor("wq", [NCH, 128, D], F32, kind="ExternalInput").ap()
    wkv = nc.dram_tensor("wkv", [2, 128, D], F32, kind="ExternalInput").ap()
    wout = nc.dram_tensor("wout", [NCH, 128, D], F32, kind="ExternalInput").ap()
    wffn = nc.dram_tensor("wffn", [NCH, 128, D], F32, kind="ExternalInput").ap()
    scale_in = nc.dram_tensor("scale_t", [128, NCH], F32, kind="ExternalInput").ap()
    cosk_in = nc.dram_tensor("cosk_t", [128, NTOK], F32, kind="ExternalInput").ap()
    sink_in = nc.dram_tensor("sink_t", [128, NTOK], F32, kind="ExternalInput").ap()
    out = nc.dram_tensor("out", [D, NTOK], F32, kind="ExternalOutput").ap()

    phases = DEV_PHASES if DEV_PHASES is not None else ["A", "B0", "C0", "D0", "B1", "C1", "D1"]

    with ExitStack() as es:
        def sb(name, shape, dtype, side=None):
            kw = {} if side is None else {"side": side}
            return es.enter_context(nc.sbuf_tensor(name, shape, dtype, **kw))

        def mksem(name):
            return SemW(es.enter_context(nc.semaphore(name)))

        PE = mksem("PE"); SC = mksem("SC"); VE = mksem("VE")
        GP = mksem("GP"); DO = mksem("DO")
        DWp = [mksem("DW0"), mksem("DW1")]
        DXI = mksem("DXI")
        DT = mksem("DT")
        DXC = mksem("DXC")
        DXC1 = mksem("DXC1")
        DXCp = [DXC, DXC1]
        BSEMS = (PE, SC, VE, GP, DWp[0], DWp[1], DXI, DT, DXC, DXC1, DO)
        BSEMS_B = (PE, SC, VE, GP, DWp[0], DWp[1], DXI, DT, DXC, DXC1)
        SERIAL = {"sems": {"vector": VE, "scalar": SC, "gpsimd": GP}, "last": {}}

        ones_sb = sb("ones_sb", [128, 128], BF16)
        ident = sb("ident", [128, 128], BF16)
        kT = sb("kT", [128, NTOK], BF16)
        v_sb = sb("v_sb", [128, NTOK], BF16)
        negM = sb("negM", [128, HEADS], F32)
        maxq2 = sb("maxq2", [128, HEADS], F32)
        kmax2 = sb("kmax2", [128, 1], F32)
        scale_sb = sb("scale_sb", [128, NCH], F32)
        eps_sb = sb("eps_sb", [128, 1], F32)
        mqp = sb("mqp", [128, NT], F32)
        pT = [sb(f"pT{i}", [128, 2 * T], BF16) for i in range(2)]
        rz = [sb(f"rz{i}", [128, T], F32) for i in range(2)]

        esq = [ExitStack(), ExitStack()]
        q_h = [None, None]
        q_h[1] = esq[1].enter_context(
            nc.sbuf_tensor("q_h1", [128, NCH * 1024], BF16, side="right"))
        q_h[0] = esq[0].enter_context(
            nc.sbuf_tensor("q_h0", [128, NCH * 1024], BF16, side="right"))

        state = {
            "git": 0,
            "chunk": 0,
            "pe_z": {},
            "ve_norm": {},
            "pe_col": {0: 0, 1: 0},
            "wpref": {},
            "xpref": {},
            "sc_x1b": {0: 0, 1: 0},
            "dapply": [],
        }

        with nc.Block() as block:
            with ExitStack() as esa:
                def sba(name, shape, dtype):
                    return esa.enter_context(nc.sbuf_tensor(name, shape, dtype))
                xb = sba("xb", [128, NCH * NTOK], BF16)
                cosk = sba("cosk", [128, NTOK], F32)
                sink = sba("sink", [128, NTOK], F32)
                coskb = sba("coskb", [128, NTOK], BF16)
                sinkb = sba("sinkb", [128, NTOK], BF16)
                qscr = [sba(f"qscr{i}", [128, T], BF16) for i in range(2)]
                qrot = [sba(f"qrot{i}", [128, T], BF16) for i in range(2)]
                srotb = sba("srotb", [128, NTOK], BF16)
                wcol_a = [sba(f"wcol_a{i}", [128, D], BF16) for i in range(2)]
                sq_scr = [sba(f"sq_scr{i}", [128, T], BF16) for i in range(2)]
                redb = [sba(f"redb{i}", [128, T], BF16) for i in range(2)]
                vtscr = [sba(f"vtscr{i}", [128, T], BF16) for i in range(2)]

                s = Sched(SERIAL)
                s.emit("vector", lambda e: e.memset(ident[:], 1.0), inc=VE)
                s.emit("vector", lambda e: e.memset(eps_sb[:], 1e-5), inc=VE)
                ve_id = VE.v
                s.emit("gpsimd", lambda e: e.memset(ones_sb[:], 1.0), inc=GP)
                s.wait("gpsimd", VE, ve_id)
                s.emit("gpsimd", lambda e: e.affine_select(
                    ident[:], ident[:], [[1, 128]], ALU.is_equal, 0.0,
                    base=0, channel_multiplier=-1), inc=GP)
                gp_consts = GP.v
                s.wait("sync", DT, DT.v)
                s.emit("sync", lambda e: e.dma_start(out=scale_sb[:], in_=scale_in),
                       inc=DT, inc_by=16)
                for tab_sb, tab_in in ((cosk, cosk_in), (sink, sink_in)):
                    s.wait("sync", DT, DT.v)
                    s.emit("sync", lambda e, tab_sb=tab_sb, tab_in=tab_in: e.dma_start(
                        out=tab_sb[:], in_=tab_in), inc=DT, inc_by=16)
                dt_tabs = DT.v
                s.wait("vector", DT, dt_tabs)
                s.emit("vector", lambda e: e.tensor_copy(coskb[:], cosk[:]))
                s.emit("vector", lambda e: e.tensor_copy(sinkb[:], sink[:]), inc=VE)
                s.emit("vector", lambda e: e.tensor_scalar(
                    out=srotb[0:64, :], in0=sinkb[0:64, :], scalar1=-1.0,
                    scalar2=None, op0=ALU.mult))
                s.emit("vector", lambda e: e.tensor_copy(
                    srotb[64:128, :], sinkb[64:128, :]), inc=VE)
                ve_tabs_b16 = VE.v

                s.emit("gpsimd", lambda e: e.dma_start(
                    out=xb[:].rearrange("p (kc t) -> p kc t", kc=NCH),
                    in_=xT.rearrange("(kc p) t -> p kc t", p=128)), inc=DXI,
                    inc_by=16, is_dma=True)
                dxi_all = DXI.v

                rope_a, rope_b = redb[0], redb[1]

                def rope_evac(ps_ap, dst_lo_fn, ctb, stb_rot, qs, qr):
                    s.emit("scalar", lambda e: e.activation(qs[:], ps_ap[:], AF.Copy),
                           inc=SC)
                    sc_cp = SC.v
                    s.wait("vector", SC, sc_cp)
                    s.emit("vector", lambda e: e.tensor_copy(qr[0:64, :], qs[64:128, :]))
                    s.emit("vector", lambda e: e.tensor_copy(qr[64:128, :], qs[0:64, :]))
                    s.emit("vector", lambda e: e.tensor_mul(rope_a[:], qs[:], ctb))
                    s.emit("vector", lambda e: e.tensor_mul(rope_b[:], qr[:], stb_rot))
                    ve = s.emit("vector", lambda e: e.tensor_add(
                        dst_lo_fn(0, 128), rope_a[:], rope_b[:]), inc=VE)
                    return sc_cp, ve

                with nc.psum_tensor("qp0", [128, T], F32) as qp0, \
                     nc.psum_tensor("qp1", [128, T], F32) as qp1, \
                     nc.psum_tensor("qp2", [128, T], F32) as qp2, \
                     nc.psum_tensor("qp3", [128, T], F32) as qp3, \
                     nc.psum_tensor("qn_pr0", [128, T], F32) as qn_pr0, \
                     nc.psum_tensor("qn_pr1", [128, T], F32) as qn_pr1, \
                     nc.psum_tensor("tp_ps", [128, 128], BF16) as tp_ps:
                    qp = [qp0, qp1, qp2, qp3]
                    qn_pr = [qn_pr0, qn_pr1]
                    ve_qred = {}
                    pe_col_last = {0: 0, 1: 0}
                    pe_qn_last = {0: 0, 1: 0}
                    ve_rope = {}

                    def col_load(src_ap, m, buf):
                        s.wait("gpsimd", PE, pe_col_last[buf])
                        s.emit("gpsimd", lambda e, src_ap=src_ap, m=m, buf=buf: e.dma_start(
                            out=wcol_a[buf][:], in_=src_ap[m]), inc=DWp[buf], inc_by=16,
                            is_dma=True)
                        return (buf, DWp[buf].v)

                    ve_maxred = {}
                    sc_qcopy = {}
                    sc_qfree = {}
                    sc_kvfree = {}
                    pend = []

                    def flush_pend(n_keep=0):
                        while len(pend) > n_keep:
                            pend.pop(0)()

                    def q_tile(m, tt, dw_need):
                        idx = m * NT + tt
                        b = idx % 4
                        lo, hi = tt * T, (tt + 1) * T
                        flush_pend(1)
                        s.wait("tensor", DWp[dw_need[0]], dw_need[1])
                        s.wait("tensor", DXI, dxi_all)
                        if idx >= 4:
                            pidx = idx - 4
                            s.wait("tensor", SC, sc_qfree[(pidx // NT, pidx % NT)])
                        for k in range(NCH):
                            s.emit("tensor", lambda e, b=b, m=m, k=k, lo=lo, hi=hi: e.matmul(
                                qp[b][:], wcol_a[m % 2][:, k * 128:(k + 1) * 128],
                                xb[:, k * NTOK + lo: k * NTOK + hi],
                                start=(k == 0), stop=(k == NCH - 1)),
                                inc=(PE if k == NCH - 1 else None))
                        pe_col_last[m % 2] = PE.v
                        pe_q = PE.v
                        qdst = q_h[0] if tt < 2 else q_h[1]
                        cbase = m * 1024 + (tt % 2) * T
                        s.wait("scalar", PE, pe_q)
                        if m == 0 and tt == 0:
                            s.wait("vector", VE, ve_tabs_b16)
                        if idx >= 2:
                            pidx = idx - 2
                            s.wait("scalar", VE, ve_rope[(pidx // NT, pidx % NT)])

                        def dstf(plo, phi, qdst=qdst, cbase=cbase):
                            return qdst[plo:phi, cbase:cbase + T]
                        sc_qcopy[(m, tt)], ve_rope[(m, tt)] = rope_evac(
                            qp[b], dstf, coskb[:, lo:hi], srotb[:, lo:hi],
                            qscr[idx % 2], qrot[idx % 2])
                        sqb = (m * NT + tt) % 2
                        s.wait("scalar", PE, pe_qn_last[sqb])
                        s.emit("scalar", lambda e, b=b, sqb=sqb: e.activation(
                            sq_scr[sqb][:], qp[b][:], AF.Square), inc=SC,
                            chain=False)
                        sc_sq = SC.v
                        sc_qfree[(m, tt)] = sc_sq

                        def qn_mm(m=m, tt=tt, idx=idx, sqb=sqb, sc_sq=sc_sq):
                            nb = idx % 2
                            s.wait("tensor", SC, sc_sq)
                            if idx >= 2:
                                s.wait("tensor", VE, ve_qred[idx - 2])
                            s.emit("tensor", lambda e, nb=nb: e.matmul(
                                qn_pr[nb][:], ones_sb[:], sq_scr[sqb][:],
                                start=True, stop=True), inc=PE)
                            pe_qn_last[sqb] = PE.v
                            s.wait("vector", PE, PE.v)
                            ve_qred[idx] = s.emit(
                                "vector", lambda e, nb=nb, tt=tt: e.tensor_reduce(
                                    mqp[:, tt:tt + 1], qn_pr[nb][:], axis=AXX,
                                    op=ALU.max), inc=VE)
                            if tt == NT - 1:
                                ve_maxred[m] = s.emit(
                                    "vector", lambda e, m=m: e.tensor_reduce(
                                        maxq2[:, m:m + 1], mqp[:], axis=AXX,
                                        op=ALU.max), inc=VE)
                        pend.append(qn_mm)

                    for m in range(NCH):
                        dw_c = col_load(wq, m, m % 2)
                        for tt in range(NT):
                            q_tile(m, tt, dw_c)
                    flush_pend(0)

                    kv_state = {"pe_kn": 0, "pe_tp_last": {}}
                    ve_kv = {}
                    sc_kvcopy = {}

                    def kv_tile(mkv, tt, dw_need):
                        idx = mkv * NT + tt
                        b = (64 + idx) % 4
                        lo, hi = tt * T, (tt + 1) * T
                        flush_pend(1)
                        s.wait("tensor", DWp[dw_need[0]], dw_need[1])
                        if idx >= 4:
                            pidx = idx - 4
                            pk = (pidx // NT, pidx % NT)
                            if pk in sc_kvfree:
                                s.wait("tensor", SC, sc_kvfree[pk])
                            else:
                                s.wait("tensor", VE, ve_kv.get(pk))
                        else:
                            s.wait("tensor", SC, sc_qfree[(NCH - 1, idx)])
                        for k in range(NCH):
                            s.emit("tensor", lambda e, b=b, mkv=mkv, k=k, lo=lo, hi=hi: e.matmul(
                                qp[b][:], wcol_a[mkv % 2][:, k * 128:(k + 1) * 128],
                                xb[:, k * NTOK + lo: k * NTOK + hi],
                                start=(k == 0), stop=(k == NCH - 1)),
                                inc=(PE if k == NCH - 1 else None))
                        pe_col_last[mkv % 2] = PE.v
                        pe_kv = PE.v
                        if mkv == 0:
                            s.wait("scalar", PE, pe_kv)
                            if idx >= 2:
                                pk = (idx - 2 // NT, (idx - 2) % NT)
                                pk = ((idx - 2) // NT, (idx - 2) % NT)
                                if pk in ve_kv and pk in sc_kvcopy:
                                    s.wait("scalar", VE, ve_kv[pk])
                            else:
                                s.wait("scalar", VE, ve_rope[(NCH - 1, 2 + idx)])

                            def dstk(plo, phi, lo=lo, hi=hi):
                                return kT[plo:phi, lo:hi]
                            sc_kvcopy[(mkv, tt)], ve_kv[(mkv, tt)] = rope_evac(
                                qp[b], dstk, coskb[:, lo:hi], srotb[:, lo:hi],
                                qscr[idx % 2], qrot[idx % 2])
                            sqb = tt % 2
                            s.wait("scalar", PE, pe_qn_last[sqb])
                            s.emit("scalar", lambda e, b=b, sqb=sqb: e.activation(
                                sq_scr[sqb][:], qp[b][:], AF.Square), inc=SC,
                                chain=False)
                            sck = SC.v
                            sc_kvfree[(mkv, tt)] = sck

                            def kn_mm(tt=tt, sqb=sqb, sck=sck):
                                gidx = 64 + tt
                                nb = gidx % 2
                                s.wait("tensor", SC, sck)
                                s.wait("tensor", VE, ve_qred[gidx - 2])
                                s.emit("tensor", lambda e, nb=nb: e.matmul(
                                    qn_pr[nb][:], ones_sb[:], sq_scr[sqb][:],
                                    start=True, stop=True), inc=PE)
                                kv_state["pe_kn"] = PE.v
                                pe_qn_last[sqb] = PE.v
                                s.wait("vector", PE, PE.v)
                                ve_qred[gidx] = s.emit(
                                    "vector", lambda e, nb=nb, tt=tt: e.tensor_reduce(
                                        mqp[:, tt:tt + 1], qn_pr[nb][:], axis=AXX,
                                        op=ALU.max), inc=VE)
                            pend.append(kn_mm)
                        else:
                            s.wait("vector", PE, pe_kv)
                            if tt >= 2:
                                s.wait("vector", PE, kv_state["pe_tp_last"].get(tt - 2, 0))
                            ve_kv[(mkv, tt)] = s.emit("vector", lambda e, b=b, tt=tt: e.tensor_copy(
                                vtscr[tt % 2][:], qp[b][:]), inc=VE)
                            ve_tp_prev = None
                            for j in range(4):
                                jj = tt * 4 + j
                                s.wait("tensor", VE, ve_kv[(mkv, tt)])
                                if ve_tp_prev is not None:
                                    s.wait("tensor", VE, ve_tp_prev)
                                elif kv_state["pe_tp_last"]:
                                    pass
                                s.emit("tensor", lambda e, tt=tt, j=j: e.transpose(
                                    tp_ps[:], vtscr[tt % 2][:, j * 128:(j + 1) * 128],
                                    ident[:]), inc=PE)
                                pe_tp = PE.v
                                s.wait("vector", PE, pe_tp)
                                ve_tp_prev = s.emit("vector", lambda e, jj=jj: e.tensor_copy(
                                    v_sb[:, jj * 128:(jj + 1) * 128], tp_ps[:]), inc=VE)
                            kv_state["pe_tp_last"][tt] = pe_tp

                    for mkv in range(2):
                        dw_c = col_load(wkv, mkv, mkv % 2)
                        for tt in range(NT):
                            kv_tile(mkv, tt, dw_c)
                    flush_pend(0)

                    s.emit("vector", lambda e: e.tensor_reduce(
                        kmax2[:], mqp[:], axis=AXX, op=ALU.max), inc=VE)
                    ve_kmax = VE.v

                    s.wait("vector", VE, ve_kmax)
                    s.emit("vector", lambda e: e.tensor_scalar(
                        out=negM[:], in0=maxq2[:], scalar1=kmax2[:], scalar2=None,
                        op0=ALU.mult), inc=VE)
                    s.wait("scalar", VE, VE.v)
                    s.emit("scalar", lambda e: e.activation(negM[:], negM[:], AF.Sqrt), inc=SC)
                    s.wait("vector", SC, SC.v)
                    s.emit("vector", lambda e: e.tensor_scalar(
                        out=negM[:], in0=negM[:], scalar1=-SCALE, scalar2=None,
                        op0=ALU.mult), inc=VE)

                    s.flush(block)
                    if DEV_TAP_FN is not None:
                        DEV_TAP_FN("A", dict(
                            nc=nc, block=block, Sched=Sched, sems=dict(
                                PE=PE, SC=SC, VE=VE, GP=GP, DO=DO),
                            kT=kT, v_sb=v_sb, negM=negM, maxq2=maxq2, kmax2=kmax2,
                            q_h=q_h, xb=xb, cosk=cosk,
                            sink=sink, scale_sb=scale_sb))

            with ExitStack() as escd:
                def sbc(name, shape, dtype):
                    return escd.enter_context(nc.sbuf_tensor(name, shape, dtype))
                wcol_c = [sbc(f"wcol_c{i}", [128, D], BF16) for i in range(2)]
                xcol = sbc("xcol", [128, 1024], F32)
                cast_w = [sbc(f"cast_w{i}", [128, 2 * T], BF16) for i in range(2)]
                sq_w = [sbc(f"sq_w{i}", [128, 2 * T], BF16) for i in range(2)]
                mean_w = sbc("mean_w", [128, 2 * T], F32)
                rstd_w = sbc("rstd_w", [128, 2 * T], F32)
                wg1 = xcol
                xcol_b = [xcol, rstd_w]
                g_w = mean_w


                def prefetch_cols(sch, src_ap, tag, h, ms):
                    for m in ms:
                        buf = m % 2
                        sch.wait("gpsimd", PE, state["pe_col"][buf])
                        sch.emit("gpsimd", lambda e, m=m, buf=buf, src_ap=src_ap:
                                 e.dma_start(out=wcol_c[buf][:], in_=src_ap[m]),
                                 inc=DWp[buf], inc_by=16, is_dma=True)
                        state["wpref"][(tag, h, m)] = (buf, DWp[buf].v)

                def phase_B(s, h, o_ps, z_ps, s_ps, oTf):
                    def one_head(t, hh):
                        g = state["git"]
                        ob = 0
                        Kt = 4 * t + 4
                        P = Kt // 2
                        base = state["chunk"]
                        qcol0 = hh * 1024 + (t % 2) * T
                        qsrc = q_h[h]
                        pe_s = {}
                        sc_exp = {}
                        gp_mask = {}

                        def emit_S(kc):
                            lo = max(0, kc - 4 * t) * 128
                            p, hf = kc // 2, kc % 2
                            s.emit("tensor", lambda e, kc=kc, lo=lo, p=p, hf=hf:
                                   e.matmul(s_ps[p % 3][:, hf * T + lo: hf * T + T],
                                            kT[:, kc * 128:(kc + 1) * 128],
                                            qsrc[:, qcol0 + lo: qcol0 + T],
                                            start=True, stop=True), inc=PE)
                            pe_s[kc] = PE.v

                        def emit_AV(kc):
                            lo = max(0, kc - 4 * t) * 128
                            n = base + kc
                            jp, hf = (n // 2) % 2, kc % 2
                            if kc >= 4 * t:
                                s.wait("tensor", GP, gp_mask[kc])
                            else:
                                s.wait("tensor", SC, sc_exp[kc])
                            if kc == 0:
                                s.wait("tensor", VE, state["ve_norm"].get(g - 1, 0))
                            s.emit("tensor", lambda e, kc=kc, lo=lo, jp=jp, hf=hf, ob=ob, Kt=Kt:
                                   e.matmul(o_ps[ob][:, lo:T],
                                            v_sb[:, kc * 128:(kc + 1) * 128],
                                            pT[jp][:, hf * T + lo: hf * T + T],
                                            start=(kc == 0), stop=(kc == Kt - 1)), inc=PE)

                        def emit_Z(kc):
                            lo = max(0, kc - 4 * t) * 128
                            n = base + kc
                            jp, hf = (n // 2) % 2, kc % 2
                            s.emit("tensor", lambda e, kc=kc, lo=lo, jp=jp, hf=hf, ob=ob, Kt=Kt:
                                   e.matmul(z_ps[ob][:, lo:T], ones_sb[:],
                                            pT[jp][:, hf * T + lo: hf * T + T],
                                            start=(kc == 0), stop=(kc == Kt - 1)), inc=PE)
                            state["pe_z"][n] = PE.v

                        def emit_exp(p):
                            n0 = base + 2 * p
                            jp = (n0 // 2) % 2
                            s.wait("scalar", PE, pe_s[2 * p + 1])
                            s.wait("scalar", PE, state["pe_z"].get(n0 - 3, 0))
                            if 2 * p >= 4 * t:
                                for hf in range(2):
                                    kc = 2 * p + hf
                                    lo = max(0, kc - 4 * t) * 128
                                    s.emit("scalar", lambda e, jp=jp, hf=hf, lo=lo, p=p, hh=hh:
                                           e.activation(pT[jp][:, hf * T + lo: hf * T + T],
                                                        s_ps[p % 3][:, hf * T + lo: hf * T + T],
                                                        AF.Exp, bias=negM[:, hh:hh + 1],
                                                        scale=SCALE), inc=SC, chain=False)
                                    sc_exp[kc] = SC.v
                            else:
                                s.emit("scalar", lambda e, jp=jp, p=p, hh=hh:
                                       e.activation(pT[jp][:, :],
                                                    s_ps[p % 3][:, :],
                                                    AF.Exp, bias=negM[:, hh:hh + 1],
                                                    scale=SCALE), inc=SC, chain=False)
                                sc_exp[2 * p] = sc_exp[2 * p + 1] = SC.v
                            for hf in range(2):
                                kc = 2 * p + hf
                                if kc >= 4 * t:
                                    lo = (kc - 4 * t) * 128
                                    s.wait("gpsimd", SC, sc_exp[kc])
                                    s.emit("gpsimd", lambda e, jp=jp, hf=hf, lo=lo:
                                           e.affine_select(
                                               pT[jp][:, hf * T + lo: hf * T + lo + 128],
                                               pT[jp][:, hf * T + lo: hf * T + lo + 128],
                                               [[1, 128]], ALU.is_ge, 0.0,
                                               base=0, channel_multiplier=-1), inc=GP,
                                           chain=False)
                                    gp_mask[kc] = GP.v

                        def emit_AVZ_pair(p):
                            emit_AV(2 * p)
                            emit_AV(2 * p + 1)
                            emit_Z(2 * p)
                            emit_Z(2 * p + 1)

                        for p in range(P):
                            emit_S(2 * p)
                            emit_S(2 * p + 1)
                            if p >= 2:
                                emit_AVZ_pair(p - 2)
                            if p >= 1:
                                emit_exp(p - 1)
                        emit_exp(P - 1)
                        emit_AVZ_pair(P - 2)
                        emit_AVZ_pair(P - 1)
                        state["chunk"] += Kt

                        ocol = hh * 1024 + (t % 2) * T
                        s.wait("scalar", PE, state["pe_z"][base + Kt - 1])
                        s.emit("scalar", lambda e, ob=ob: e.activation(
                            rz[ob][:], z_ps[ob][:], AF.Ln), chain=False)
                        sc_rz = s.emit("scalar", lambda e, ob=ob: e.activation(
                            rz[ob][:], rz[ob][:], AF.Exp, scale=-1.0), inc=SC)
                        s.wait("vector", SC, sc_rz)
                        state["ve_norm"][g] = s.emit(
                            "vector", lambda e, ob=ob, ocol=ocol: e.tensor_mul(
                                oTf[:, ocol:ocol + T], o_ps[ob][:], rz[ob][:]), inc=VE,
                            chain=False)
                        state["git"] += 1
                        if state["dapply"]:
                            state["dapply"].pop(0)(s)

                    for t in (2 * h, 2 * h + 1):
                        for hh in range(HEADS):
                            one_head(t, hh)
                    while state["dapply"]:
                        state["dapply"].pop(0)(s)

                def phase_C(s, h, y_ps, mn_ps, sq_ps, oTf, s_h, x1b_h):
                    pe_y = {}
                    ve_evac = {}
                    pe_stats = {}

                    def one_m(m):
                        buf = m % 2
                        if ("C", h, m) in state["wpref"]:
                            dw_c = state["wpref"].pop(("C", h, m))
                        else:
                            s.wait("gpsimd", PE, state["pe_col"][buf])
                            s.emit("gpsimd", lambda e, m=m, buf=buf: e.dma_start(
                                out=wcol_c[buf][:], in_=wout[m]), inc=DWp[buf], inc_by=16,
                                is_dma=True)
                            dw_c = (buf, DWp[buf].v)
                        xb2 = m % 2
                        if m == 0 and h in state["xpref"]:
                            dx_x = state["xpref"].pop(h)
                        else:
                            if m > 1:
                                s.wait("sync", VE, ve_evac[(m - 2, 1)])
                            s.emit("sync", lambda e, m=m, h=h, xb2=xb2: e.dma_start(
                                out=xcol_b[xb2][:],
                                in_=xT[m * 128:(m + 1) * 128, h * 1024:(h + 1) * 1024]),
                                inc=DXCp[xb2], inc_by=16)
                            dx_x = DXCp[xb2].v
                        for tt in range(2):
                            b2 = (2 * m + tt) % 2
                            idx = 2 * m + tt
                            flush_pend(1)
                            s.wait("tensor", DWp[dw_c[0]], dw_c[1])
                            if idx >= 2:
                                s.wait("tensor", VE, ve_evac[((idx - 2) // 2, (idx - 2) % 2)])
                            for k in range(NCH):
                                s.emit("tensor", lambda e, b2=b2, m=m, k=k, tt=tt: e.matmul(
                                    y_ps[b2][:], wcol_c[m % 2][:, k * 128:(k + 1) * 128],
                                    oTf[:, k * 1024 + tt * T: k * 1024 + (tt + 1) * T],
                                    start=(k == 0), stop=(k == NCH - 1)),
                                    inc=(PE if k == NCH - 1 else None))
                            pe_y[(m, tt)] = PE.v
                            state["pe_col"][m % 2] = PE.v
                            scol = m * 1024 + tt * T
                            s.wait("vector", PE, pe_y[(m, tt)])
                            s.wait("vector", DXCp[xb2], dx_x)
                            ve_evac[(m, tt)] = s.emit(
                                "vector", lambda e, b2=b2, scol=scol, tt=tt, xb2=xb2:
                                e.tensor_add(
                                    s_h[:, scol:scol + T], y_ps[b2][:],
                                    xcol_b[xb2][:, tt * T:(tt + 1) * T]), inc=VE,
                                chain=False)
                            s.wait("scalar", VE, ve_evac[(m, tt)])
                            if idx >= 4:
                                s.wait("scalar", PE, pe_stats[((idx - 4) // 2, (idx - 4) % 2)])
                            wb, wt = (idx // 2) % 2, idx % 2
                            s.emit("scalar", lambda e, wb=wb, wt=wt, scol=scol: e.activation(
                                cast_w[wb][:, wt * T:(wt + 1) * T],
                                s_h[:, scol:scol + T], AF.Copy), chain=False)
                            s.emit("scalar", lambda e, wb=wb, wt=wt, scol=scol: e.activation(
                                sq_w[wb][:, wt * T:(wt + 1) * T],
                                s_h[:, scol:scol + T], AF.Square), inc=SC, chain=False)
                            sc_c = SC.v

                            def st_mm(m=m, tt=tt, wb=wb, wt=wt, sc_c=sc_c):
                                s.wait("tensor", SC, sc_c)
                                s.emit("tensor", lambda e, tt=tt: e.matmul(
                                    mn_ps[:, tt * T:(tt + 1) * T], ones_sb[:],
                                    cast_w[wb][:, wt * T:(wt + 1) * T],
                                    start=(m == 0), stop=(m == NCH - 1)), inc=PE)
                                s.emit("tensor", lambda e, tt=tt: e.matmul(
                                    sq_ps[:, tt * T:(tt + 1) * T], ones_sb[:],
                                    sq_w[wb][:, wt * T:(wt + 1) * T],
                                    start=(m == 0), stop=(m == NCH - 1)), inc=PE)
                                pe_stats[(m, tt)] = PE.v
                            pend.append(st_mm)

                    pend = []

                    def flush_pend(n_keep=0):
                        while len(pend) > n_keep:
                            pend.pop(0)()

                    for m in range(NCH):
                        one_m(m)
                    flush_pend(0)
                    prefetch_cols(s, wffn, "D", h, [0, 1])

                    if DEV_C_STOP == "mm":
                        return
                    ve_rstd = ln_stats(s, mn_ps[:], sq_ps[:], mean_w[:],
                                       rstd_w[:], pe_stats[(NCH - 1, 1)], wg1)
                    if DEV_C_STOP == "stats":
                        return

                    sc_x1b = {}
                    for tt in range(2):
                        for m in range(NCH):
                            scol = m * 1024 + tt * T
                            s.wait("vector", ve_rstd[0], ve_rstd[1])
                            s.emit("vector", lambda e, scol=scol, tt=tt: e.tensor_sub(
                                s_h[:, scol:scol + T], s_h[:, scol:scol + T],
                                mean_w[:, tt * T:(tt + 1) * T]), chain=False)
                            vex = s.emit("vector", lambda e, scol=scol, tt=tt: e.tensor_mul(
                                s_h[:, scol:scol + T], s_h[:, scol:scol + T],
                                rstd_w[:, tt * T:(tt + 1) * T]), inc=VE)
                            s.wait("scalar", VE, vex)
                            s.emit("scalar", lambda e, scol=scol: e.activation(
                                x1b_h[:, scol:scol + T], s_h[:, scol:scol + T],
                                AF.Copy), inc=SC, chain=False)
                        sc_x1b[tt] = SC.v
                    state["sc_x1b"] = sc_x1b
                    state["c_ve"] = VE.v
                    state["c_gp"] = GP.v

                def ln_stats(s, m_ps, q_ps, m_sb, r_sb, pe_done, scr):
                    s.wait("vector", PE, pe_done)
                    s.emit("vector", lambda e: e.tensor_scalar(
                        out=m_sb, in0=m_ps, scalar1=1.0 / D, scalar2=None,
                        op0=ALU.mult))
                    s.emit("vector", lambda e: e.scalar_tensor_tensor(
                        out=r_sb, in0=m_ps, scalar=1.0 / D, in1=m_sb,
                        op0=ALU.mult, op1=ALU.mult))
                    s.emit("vector", lambda e: e.scalar_tensor_tensor(
                        out=scr[:], in0=q_ps, scalar=1.0 / D, in1=r_sb,
                        op0=ALU.mult, op1=ALU.subtract), inc=VE)
                    s.wait("scalar", VE, VE.v)
                    s.emit("scalar", lambda e: e.activation(
                        r_sb, scr[:], AF.Ln, bias=eps_sb[:, 0:1]))
                    scv = s.emit("scalar", lambda e: e.activation(
                        r_sb, r_sb, AF.Exp, scale=-0.5), inc=SC)
                    return (SC, scv)

                def phase_D(s, h, f_ps, fsq_ps, mn_ps, sq_ps, f_h, s_h, x1b_h,
                            defer_apply=False):
                    pe_f = {}
                    sc_fe = {}
                    pe_fsq = {}

                    def one_m(m):
                        buf = m % 2
                        if ("D", h, m) in state["wpref"]:
                            dw_c = state["wpref"].pop(("D", h, m))
                        else:
                            s.wait("gpsimd", PE, state["pe_col"][buf])
                            s.emit("gpsimd", lambda e, m=m, buf=buf: e.dma_start(
                                out=wcol_c[buf][:], in_=wffn[m]), inc=DWp[buf], inc_by=16,
                                is_dma=True)
                            dw_c = (buf, DWp[buf].v)
                        for tt in range(2):
                            b2 = (2 * m + tt) % 2
                            idx = 2 * m + tt
                            flush_pend(1)
                            s.wait("tensor", DWp[dw_c[0]], dw_c[1])
                            s.wait("tensor", SC, state["sc_x1b"][tt])
                            if idx >= 2:
                                s.wait("tensor", SC, sc_fe[idx - 2])
                            for k in range(NCH):
                                s.emit("tensor", lambda e, b2=b2, m=m, k=k, tt=tt: e.matmul(
                                    f_ps[b2][:], wcol_c[m % 2][:, k * 128:(k + 1) * 128],
                                    x1b_h[:, k * 1024 + tt * T: k * 1024 + (tt + 1) * T],
                                    start=(k == 0), stop=(k == NCH - 1)),
                                    inc=(PE if k == NCH - 1 else None))
                            pe_f[idx] = PE.v
                            state["pe_col"][m % 2] = PE.v
                            fcol = m * 1024 + tt * T
                            s.wait("scalar", PE, pe_f[idx])
                            if idx >= 4:
                                s.wait("scalar", PE, pe_fsq[idx - 4])
                            wb, wt = (idx // 2) % 2, idx % 2
                            s.emit("scalar", lambda e, b2=b2, fcol=fcol, m=m: e.activation(
                                f_h[:, fcol:fcol + T], f_ps[b2][:], AF.Copy,
                                scale=scale_sb[:, m:m + 1]), chain=False)
                            s.emit("scalar", lambda e, wb=wb, wt=wt, fcol=fcol: e.activation(
                                sq_w[wb][:, wt * T:(wt + 1) * T],
                                f_h[:, fcol:fcol + T], AF.Square), inc=SC)
                            sc_fe[idx] = SC.v

                            def fsq_mm(idx=idx, m=m, tt=tt, wb=wb, wt=wt, sc=sc_fe[idx]):
                                s.wait("tensor", SC, sc)
                                s.emit("tensor", lambda e, tt=tt: e.matmul(
                                    fsq_ps[:, tt * T:(tt + 1) * T], ones_sb[:],
                                    sq_w[wb][:, wt * T:(wt + 1) * T],
                                    start=(m == 0), stop=(m == NCH - 1)), inc=PE)
                                pe_fsq[idx] = PE.v
                            pend.append(fsq_mm)

                    pend = []

                    def flush_pend(n_keep=0):
                        while len(pend) > n_keep:
                            pend.pop(0)()

                    for m in range(NCH):
                        one_m(m)
                    flush_pend(0)

                    s.wait("scalar", PE, pe_fsq[2 * NCH - 1])
                    s.wait("scalar", GP, state["c_gp"])
                    s.wait("vector", GP, state["c_gp"])
                    s.emit("scalar", lambda e: e.activation(
                        wg1[:], fsq_ps[:], AF.Sqrt), inc=SC)
                    s.wait("vector", SC, SC.v)
                    s.emit("vector", lambda e: e.tensor_scalar(
                        out=wg1[:], in0=wg1[:], scalar1=1e-7, scalar2=None,
                        op0=ALU.max), inc=VE)
                    s.wait("scalar", VE, VE.v)
                    s.emit("scalar", lambda e: e.activation(
                        rstd_w[:], wg1[:], AF.Tanh), inc=SC)
                    s.emit("vector", lambda e: e.tensor_scalar(
                        out=g_w[:], in0=wg1[:], scalar1=1e-7, scalar2=None,
                        op0=ALU.add), inc=VE)
                    s.wait("scalar", VE, VE.v)
                    s.emit("scalar", lambda e: e.activation(g_w[:], g_w[:], AF.Ln))
                    s.emit("scalar", lambda e: e.activation(
                        g_w[:], g_w[:], AF.Exp, scale=-1.0), inc=SC)
                    s.wait("vector", SC, SC.v)
                    s.emit("vector", lambda e: e.tensor_mul(g_w[:], g_w[:], rstd_w[:]))
                    s.emit("vector", lambda e: e.tensor_mul(rstd_w[:], wg1[:], g_w[:]))
                    s.emit("vector", lambda e: e.tensor_scalar(
                        out=rstd_w[:], in0=rstd_w[:], scalar1=1e-7, scalar2=None,
                        op0=ALU.max), inc=VE)
                    s.wait("scalar", VE, VE.v)
                    s.emit("scalar", lambda e: e.activation(rstd_w[:], rstd_w[:], AF.Ln))
                    s.emit("scalar", lambda e: e.activation(
                        rstd_w[:], rstd_w[:], AF.Exp, scale=-1.0), inc=SC)
                    s.wait("vector", SC, SC.v)
                    s.emit("vector", lambda e: e.tensor_scalar(
                        out=rstd_w[:], in0=rstd_w[:], scalar1=0.99, scalar2=1.0,
                        op0=ALU.mult, op1=ALU.min))
                    ve_gate = s.emit("vector", lambda e: e.tensor_mul(
                        g_w[:], g_w[:], rstd_w[:]), inc=VE)

                    pe_stats2 = {}
                    pend2 = []

                    def st2_mm(m, b2, sc_cw):
                        s.wait("tensor", SC, sc_cw)
                        for tt in range(2):
                            s.emit("tensor", lambda e, b2=b2, tt=tt, m=m: e.matmul(
                                mn_ps[:, tt * T:(tt + 1) * T], ones_sb[:],
                                cast_w[b2][:, tt * T:(tt + 1) * T],
                                start=(m == 0), stop=(m == NCH - 1)))
                            s.emit("tensor", lambda e, b2=b2, tt=tt, m=m: e.matmul(
                                sq_ps[:, tt * T:(tt + 1) * T], ones_sb[:],
                                sq_w[b2][:, tt * T:(tt + 1) * T],
                                start=(m == 0), stop=(m == NCH - 1)),
                                inc=(PE if tt == 1 else None))
                        pe_stats2[m] = PE.v

                    for m in range(NCH):
                        b2 = m % 2
                        wlo = m * 1024
                        while len(pend2) > 1:
                            pend2.pop(0)()
                        s.emit("vector", lambda e, wlo=wlo: e.tensor_mul(
                            wg1[:], f_h[:, wlo:wlo + 1024], g_w[:]))
                        ve_s2 = s.emit("vector", lambda e, wlo=wlo: e.tensor_add(
                            s_h[:, wlo:wlo + 1024], s_h[:, wlo:wlo + 1024],
                            wg1[:]), inc=VE)
                        s.wait("scalar", VE, ve_s2)
                        if m >= 2:
                            s.wait("scalar", PE, pe_stats2[m - 2])
                        s.emit("scalar", lambda e, b2=b2, wlo=wlo: e.activation(
                            cast_w[b2][:], s_h[:, wlo:wlo + 1024], AF.Copy), chain=False)
                        sc_cw = s.emit("scalar", lambda e, b2=b2, wlo=wlo: e.activation(
                            sq_w[b2][:], s_h[:, wlo:wlo + 1024], AF.Square), inc=SC,
                            chain=False)
                        pend2.append(lambda m=m, b2=b2, sc_cw=sc_cw: st2_mm(m, b2, sc_cw))
                    while pend2:
                        pend2.pop(0)()

                    ve_rstd2 = ln_stats(s, mn_ps[:], sq_ps[:], mean_w[:],
                                        rstd_w[:], pe_stats2[NCH - 1], wg1)

                    def emit_apply(sb, m):
                        wlo = m * 1024
                        sb.wait("vector", ve_rstd2[0], ve_rstd2[1])
                        sb.emit("vector", lambda e, wlo=wlo: e.tensor_sub(
                            s_h[:, wlo:wlo + 1024], s_h[:, wlo:wlo + 1024],
                            mean_w[:]), chain=False)
                        vex = sb.emit("vector", lambda e, wlo=wlo: e.tensor_mul(
                            s_h[:, wlo:wlo + 1024], s_h[:, wlo:wlo + 1024],
                            rstd_w[:]), inc=VE)
                        sb.wait("sync", VE, vex)
                        sb.emit("sync", lambda e, m=m, h=h, wlo=wlo: e.dma_start(
                            out=out[m * 128:(m + 1) * 128,
                                    h * 1024:(h + 1) * 1024],
                            in_=s_h[:, wlo:wlo + 1024]), inc=DO, inc_by=16)

                    if defer_apply:
                        for m in range(NCH):
                            state["dapply"].append(
                                lambda sb, m=m: emit_apply(sb, m))
                    else:
                        for m in range(NCH):
                            emit_apply(s, m)

                ess = [ExitStack(), ExitStack()]
                for h in range(2):
                    with ExitStack() as esh:
                        oTf = esh.enter_context(
                            nc.sbuf_tensor(f"oTf{h}", [128, NCH * 1024], BF16))
                        f_h = oTf

                        if f"B{h}" in phases:
                            s = Sched(SERIAL)
                            s.barrier_in(("tensor", "scalar", "vector", "gpsimd", "sync"), BSEMS_B)
                            with nc.psum_tensor(f"o_ps{h}a", [128, T], F32) as opa, \
                                 nc.psum_tensor(f"z_ps{h}a", [128, T], F32) as zpa, \
                                 nc.psum_tensor(f"s_ps{h}a", [128, 2 * T], F32) as spa, \
                                 nc.psum_tensor(f"s_ps{h}b", [128, 2 * T], F32) as spb, \
                                 nc.psum_tensor(f"s_ps{h}c", [128, 2 * T], F32) as spc:
                                phase_B(s, h, [opa], [zpa], [spa, spb, spc], oTf)
                                prefetch_cols(s, wout, "C", h, [0, 1])
                                s.emit("sync", lambda e, h=h: e.dma_start(
                                    out=xcol[:],
                                    in_=xT[0:128, h * 1024:(h + 1) * 1024]),
                                    inc=DXC, inc_by=16)
                                state["xpref"][h] = DXC.v
                                s.flush(block)
                            if DEV_TAP_FN is not None:
                                DEV_TAP_FN(f"B{h}", dict(
                                    nc=nc, block=block, Sched=Sched, sems=dict(
                                        PE=PE, SC=SC, VE=VE, GP=GP, DO=DO),
                                    oTf=oTf))

                        if h == 1:
                            ess[0].close()
                        esq[h].close()
                        s_h = ess[h].enter_context(
                            nc.sbuf_tensor(f"s_h{h}", [128, NCH * 1024], F32,
                                           side="right"))
                        x1b_h = esh.enter_context(
                            nc.sbuf_tensor(f"x1b{h}", [128, NCH * 1024], BF16))

                        if f"C{h}" in phases:
                            s = Sched(SERIAL)
                            s.barrier_in(("tensor", "scalar", "vector", "gpsimd", "sync"), BSEMS)
                            with nc.psum_tensor(f"y_ps{h}a", [128, T], F32) as ypa, \
                                 nc.psum_tensor(f"y_ps{h}b", [128, T], F32) as ypb, \
                                 nc.psum_tensor(f"mn_ps{h}", [128, 2 * T], F32) as mpw, \
                                 nc.psum_tensor(f"sx_ps{h}", [128, 2 * T], F32) as sxw:
                                phase_C(s, h, [ypa, ypb], mpw, sxw,
                                        oTf, s_h, x1b_h)
                                s.flush(block)
                            if DEV_TAP_FN is not None:
                                DEV_TAP_FN(f"C{h}", dict(
                                    nc=nc, block=block, Sched=Sched, sems=dict(
                                        PE=PE, SC=SC, VE=VE, GP=GP, DO=DO),
                                    oTf=oTf, s_h=s_h, x1b_h=x1b_h))

                        if f"D{h}" in phases:
                            s = Sched(SERIAL)
                            with nc.psum_tensor(f"f_ps{h}a", [128, T], F32) as fpa, \
                                 nc.psum_tensor(f"f_ps{h}b", [128, T], F32) as fpb, \
                                 nc.psum_tensor(f"fq_ps{h}", [128, 2 * T], F32) as fqw, \
                                 nc.psum_tensor(f"m2_ps{h}", [128, 2 * T], F32) as m2w, \
                                 nc.psum_tensor(f"q2_ps{h}", [128, 2 * T], F32) as q2w:
                                phase_D(s, h, [fpa, fpb], fqw,
                                        m2w, q2w, f_h, s_h, x1b_h,
                                        defer_apply=(h == 0 and "B1" in phases))
                                s.flush(block)

                ess[1].close()
                s = Sched(SERIAL)
                s.wait("sync", DO, DO.v)
                s.flush(block)

    return nc


def _host_tables():
    try:
        import jax
        import jax.numpy as jnp
        cpu = jax.devices("cpu")[0]
        with jax.default_device(cpu):
            e = jnp.arange(0, 128, 2, dtype=jnp.float32) / 128.0
            inv = 1.0 / 10000.0 ** e
            freqs = jnp.arange(NTOK, dtype=jnp.float32)[:, None] * inv[None, :]
            pos = jnp.concatenate((freqs, freqs), axis=-1)
            cos = np.asarray(jnp.cos(pos))
            sin = np.asarray(jnp.sin(pos))
    except Exception:
        inv = (1.0 / 10000.0 ** (np.arange(0, 128, 2, dtype=np.float64) / 128.0)).astype(np.float32)
        f = (np.arange(NTOK, dtype=np.float32)[:, None] * inv[None, :]).astype(np.float32)
        pos = np.concatenate([f, f], axis=-1)
        cos = np.cos(pos).astype(np.float32)
        sin = np.sin(pos).astype(np.float32)
    return np.ascontiguousarray(cos.T), np.ascontiguousarray(sin.T)


def _re_w(W):
    W = np.asarray(W, np.float32)
    ncols = W.shape[1] // 128
    return np.ascontiguousarray(
        W.reshape(16, 128, ncols, 128).transpose(2, 1, 0, 3)
    ).reshape(ncols, 128, 2048)


def kernel(L0_attn_x, L0_attn_Wq, L0_attn_Wkv, L0_attn_Wout, L0_ffn_weight, L0_ffn_scale):
    x = np.asarray(L0_attn_x, np.float32)
    B = x.shape[0]
    WQ = _re_w(L0_attn_Wq)
    WKV = _re_w(L0_attn_Wkv)
    WOUT = _re_w(L0_attn_Wout)
    WFFN = _re_w(L0_ffn_weight)
    scale_t = np.ascontiguousarray(np.asarray(L0_ffn_scale, np.float32).reshape(16, 128).T)
    cosk_t, sink_t = _host_tables()

    nc = build_nc()
    in_maps = []
    for b in range(B):
        in_maps.append({
            "xT": np.ascontiguousarray(x[b].T),
            "wq": WQ, "wkv": WKV, "wout": WOUT, "wffn": WFFN,
            "scale_t": scale_t, "cosk_t": cosk_t, "sink_t": sink_t,
        })
    res = run_bass_kernel_spmd(nc, in_maps, core_ids=list(range(B)))
    return np.stack([np.ascontiguousarray(np.asarray(res.results[b]["out"]).T)
                     for b in range(B)])


# version 40
# speedup vs baseline: 1.0316x; 1.0062x over previous
import math
import numpy as np
import concourse.bass as bass
import concourse.mybir as mybir
from concourse.bass_utils import run_bass_kernel_spmd
from contextlib import ExitStack

F32 = mybir.dt.float32
BF16 = mybir.dt.bfloat16
AF = mybir.ActivationFunctionType
ALU = mybir.AluOpType
AXX = mybir.AxisListType.X

D = 2048
NTOK = 2048
NCH = 16
HEADS = 16
T = 512
NT = 4
SCALE = 128.0 ** -0.5
TWO_PI = 2.0 * math.pi
C_INVF = -math.log(10000.0) / 64.0

DEV_PHASES = None
DEV_TAP_FN = None
DEV_C_STOP = None


class SemW:
    __slots__ = ("h", "v")

    def __init__(self, h):
        self.h = h
        self.v = 0


class Sched:
    ENGINES = ("tensor", "vector", "scalar", "gpsimd", "sync")
    SERIAL = {"vector", "scalar", "gpsimd"}

    def __init__(self, serial=None):
        self.ops = {e: [] for e in self.ENGINES}
        self.serial = serial

    def emit(self, eng, fn, inc=None, inc_by=1, is_dma=False, chain=True):
        chained = (self.serial is not None and eng in self.SERIAL
                   and eng in self.serial["sems"])
        if chained:
            prev = self.serial["last"].get(eng)
            if prev is not None and chain:
                self.wait(eng, prev[0], prev[1])
            if inc is None and not is_dma:
                inc = self.serial["sems"][eng]
                inc_by = 1

        def op(e, fn=fn, inc=inc, inc_by=inc_by):
            ins = fn(e)
            if inc is not None:
                ins.then_inc(inc.h, inc_by)
        self.ops[eng].append(op)
        if inc is not None:
            inc.v += inc_by
            if chained and not is_dma:
                self.serial["last"][eng] = (inc, inc.v)
            return inc.v
        return None

    def wait(self, eng, sem, val):
        if val and val > 0:
            self.ops[eng].append(lambda e, s=sem, v=val: e.wait_ge(s.h, v))

    def barrier_in(self, engines, sems):
        for eng in engines:
            for s in sems:
                self.wait(eng, s, s.v)

    def flush(self, block):
        for name in self.ENGINES:
            lst = self.ops[name]
            if not lst:
                continue
            getattr(block, name)(lambda e, lst=lst: [f(e) for f in lst])
        self.ops = {e: [] for e in self.ENGINES}


def build_nc():
    nc = bass.Bass("TRN2", target_bir_lowering=False, debug=False)

    xT = nc.dram_tensor("xT", [D, NTOK], F32, kind="ExternalInput").ap()
    wq = nc.dram_tensor("wq", [NCH, 128, D], F32, kind="ExternalInput").ap()
    wkv = nc.dram_tensor("wkv", [2, 128, D], F32, kind="ExternalInput").ap()
    wout = nc.dram_tensor("wout", [NCH, 128, D], F32, kind="ExternalInput").ap()
    wffn = nc.dram_tensor("wffn", [NCH, 128, D], F32, kind="ExternalInput").ap()
    scale_in = nc.dram_tensor("scale_t", [128, NCH], F32, kind="ExternalInput").ap()
    cosk_in = nc.dram_tensor("cosk_t", [128, NTOK], F32, kind="ExternalInput").ap()
    sink_in = nc.dram_tensor("sink_t", [128, NTOK], F32, kind="ExternalInput").ap()
    out = nc.dram_tensor("out", [D, NTOK], F32, kind="ExternalOutput").ap()

    phases = DEV_PHASES if DEV_PHASES is not None else ["A", "B0", "C0", "D0", "B1", "C1", "D1"]

    with ExitStack() as es:
        def sb(name, shape, dtype, side=None):
            kw = {} if side is None else {"side": side}
            return es.enter_context(nc.sbuf_tensor(name, shape, dtype, **kw))

        def mksem(name):
            return SemW(es.enter_context(nc.semaphore(name)))

        PE = mksem("PE"); SC = mksem("SC"); VE = mksem("VE")
        GP = mksem("GP"); DO = mksem("DO")
        DWp = [mksem("DW0"), mksem("DW1")]
        DXI = mksem("DXI")
        DT = mksem("DT")
        DXC = mksem("DXC")
        DXC1 = mksem("DXC1")
        DXCp = [DXC, DXC1]
        BSEMS = (PE, SC, VE, GP, DWp[0], DWp[1], DXI, DT, DXC, DXC1, DO)
        BSEMS_B = (PE, SC, VE, GP, DWp[0], DWp[1], DXI, DT, DXC, DXC1)
        SERIAL = {"sems": {"vector": VE, "scalar": SC, "gpsimd": GP}, "last": {}}

        ones_sb = sb("ones_sb", [128, 128], BF16)
        ident = sb("ident", [128, 128], BF16)
        kT = sb("kT", [128, NTOK], BF16)
        v_sb = sb("v_sb", [128, NTOK], BF16)
        negM = sb("negM", [128, HEADS], F32)
        maxq2 = sb("maxq2", [128, HEADS], F32)
        kmax2 = sb("kmax2", [128, 1], F32)
        scale_sb = sb("scale_sb", [128, NCH], F32)
        eps_sb = sb("eps_sb", [128, 1], F32)
        mqp = sb("mqp", [128, NT], F32)
        pT = [sb(f"pT{i}", [128, 2 * T], BF16) for i in range(2)]
        rz = [sb(f"rz{i}", [128, T], F32) for i in range(2)]

        esq = [ExitStack(), ExitStack()]
        q_h = [None, None]
        q_h[1] = esq[1].enter_context(
            nc.sbuf_tensor("q_h1", [128, NCH * 1024], BF16, side="right"))
        q_h[0] = esq[0].enter_context(
            nc.sbuf_tensor("q_h0", [128, NCH * 1024], BF16, side="right"))

        state = {
            "git": 0,
            "chunk": 0,
            "pe_z": {},
            "ve_norm": {},
            "pe_col": {0: 0, 1: 0},
            "wpref": {},
            "xpref": {},
            "sc_x1b": {0: 0, 1: 0},
            "dapply": [],
        }

        with nc.Block() as block:
            with ExitStack() as esa:
                def sba(name, shape, dtype):
                    return esa.enter_context(nc.sbuf_tensor(name, shape, dtype))
                xb = sba("xb", [128, NCH * NTOK], BF16)
                cosk = sba("cosk", [128, NTOK], F32)
                sink = sba("sink", [128, NTOK], F32)
                coskb = sba("coskb", [128, NTOK], BF16)
                sinkb = sba("sinkb", [128, NTOK], BF16)
                qscr = [sba(f"qscr{i}", [128, T], BF16) for i in range(2)]
                qrot = [sba(f"qrot{i}", [128, T], BF16) for i in range(2)]
                srotb = sba("srotb", [128, NTOK], BF16)
                wcol_a = [sba(f"wcol_a{i}", [128, D], BF16) for i in range(2)]
                sq_scr = [sba(f"sq_scr{i}", [128, T], BF16) for i in range(2)]
                redb = [sba(f"redb{i}", [128, T], BF16) for i in range(2)]
                vtscr = [sba(f"vtscr{i}", [128, T], BF16) for i in range(2)]

                s = Sched(SERIAL)
                s.emit("vector", lambda e: e.memset(ident[:], 1.0), inc=VE)
                s.emit("vector", lambda e: e.memset(eps_sb[:], 1e-5), inc=VE)
                ve_id = VE.v
                s.emit("gpsimd", lambda e: e.memset(ones_sb[:], 1.0), inc=GP)
                s.wait("gpsimd", VE, ve_id)
                s.emit("gpsimd", lambda e: e.affine_select(
                    ident[:], ident[:], [[1, 128]], ALU.is_equal, 0.0,
                    base=0, channel_multiplier=-1), inc=GP)
                gp_consts = GP.v
                s.wait("sync", DT, DT.v)
                s.emit("sync", lambda e: e.dma_start(out=scale_sb[:], in_=scale_in),
                       inc=DT, inc_by=16)
                for tab_sb, tab_in in ((cosk, cosk_in), (sink, sink_in)):
                    s.wait("sync", DT, DT.v)
                    s.emit("sync", lambda e, tab_sb=tab_sb, tab_in=tab_in: e.dma_start(
                        out=tab_sb[:], in_=tab_in), inc=DT, inc_by=16)
                dt_tabs = DT.v
                s.wait("vector", DT, dt_tabs)
                s.emit("vector", lambda e: e.tensor_copy(coskb[:], cosk[:]))
                s.emit("vector", lambda e: e.tensor_copy(sinkb[:], sink[:]), inc=VE)
                s.emit("vector", lambda e: e.tensor_scalar(
                    out=srotb[0:64, :], in0=sinkb[0:64, :], scalar1=-1.0,
                    scalar2=None, op0=ALU.mult))
                s.emit("vector", lambda e: e.tensor_copy(
                    srotb[64:128, :], sinkb[64:128, :]), inc=VE)
                ve_tabs_b16 = VE.v

                s.emit("gpsimd", lambda e: e.dma_start(
                    out=xb[:].rearrange("p (kc t) -> p kc t", kc=NCH),
                    in_=xT.rearrange("(kc p) t -> p kc t", p=128)), inc=DXI,
                    inc_by=16, is_dma=True)
                dxi_all = DXI.v

                rope_a, rope_b = redb[0], redb[1]

                def rope_evac(ps_ap, dst_lo_fn, ctb, stb_rot, qs, qr):
                    s.emit("scalar", lambda e: e.activation(qs[:], ps_ap[:], AF.Copy),
                           inc=SC)
                    sc_cp = SC.v
                    s.wait("vector", SC, sc_cp)
                    s.emit("vector", lambda e: e.tensor_copy(qr[0:64, :], qs[64:128, :]))
                    s.emit("vector", lambda e: e.tensor_copy(qr[64:128, :], qs[0:64, :]))
                    s.emit("vector", lambda e: e.tensor_mul(rope_a[:], qs[:], ctb))
                    s.emit("vector", lambda e: e.tensor_mul(rope_b[:], qr[:], stb_rot))
                    ve = s.emit("vector", lambda e: e.tensor_add(
                        dst_lo_fn(0, 128), rope_a[:], rope_b[:]), inc=VE)
                    return sc_cp, ve

                with nc.psum_tensor("qp0", [128, T], F32) as qp0, \
                     nc.psum_tensor("qp1", [128, T], F32) as qp1, \
                     nc.psum_tensor("qp2", [128, T], F32) as qp2, \
                     nc.psum_tensor("qp3", [128, T], F32) as qp3, \
                     nc.psum_tensor("qn_pr0", [128, T], F32) as qn_pr0, \
                     nc.psum_tensor("qn_pr1", [128, T], F32) as qn_pr1, \
                     nc.psum_tensor("tp_ps0", [128, 128], BF16) as tp_ps0, \
                     nc.psum_tensor("tp_ps1", [128, 128], BF16) as tp_ps1:
                    qp = [qp0, qp1, qp2, qp3]
                    qn_pr = [qn_pr0, qn_pr1]
                    tp_psb = [tp_ps0, tp_ps1]
                    ve_qred = {}
                    pe_col_last = {0: 0, 1: 0}
                    pe_qn_last = {0: 0, 1: 0}
                    ve_rope = {}

                    def col_load(src_ap, m, buf):
                        s.wait("gpsimd", PE, pe_col_last[buf])
                        s.emit("gpsimd", lambda e, src_ap=src_ap, m=m, buf=buf: e.dma_start(
                            out=wcol_a[buf][:], in_=src_ap[m]), inc=DWp[buf], inc_by=16,
                            is_dma=True)
                        return (buf, DWp[buf].v)

                    ve_maxred = {}
                    sc_qcopy = {}
                    sc_qfree = {}
                    sc_kvfree = {}
                    pend = []

                    def flush_pend(n_keep=0):
                        while len(pend) > n_keep:
                            pend.pop(0)()

                    def q_tile(m, tt, dw_need):
                        idx = m * NT + tt
                        b = idx % 4
                        lo, hi = tt * T, (tt + 1) * T
                        flush_pend(1)
                        s.wait("tensor", DWp[dw_need[0]], dw_need[1])
                        s.wait("tensor", DXI, dxi_all)
                        if idx >= 4:
                            pidx = idx - 4
                            s.wait("tensor", SC, sc_qfree[(pidx // NT, pidx % NT)])
                        for k in range(NCH):
                            s.emit("tensor", lambda e, b=b, m=m, k=k, lo=lo, hi=hi: e.matmul(
                                qp[b][:], wcol_a[m % 2][:, k * 128:(k + 1) * 128],
                                xb[:, k * NTOK + lo: k * NTOK + hi],
                                start=(k == 0), stop=(k == NCH - 1)),
                                inc=(PE if k == NCH - 1 else None))
                        pe_col_last[m % 2] = PE.v
                        pe_q = PE.v
                        qdst = q_h[0] if tt < 2 else q_h[1]
                        cbase = m * 1024 + (tt % 2) * T
                        s.wait("scalar", PE, pe_q)
                        if m == 0 and tt == 0:
                            s.wait("vector", VE, ve_tabs_b16)
                        if idx >= 2:
                            pidx = idx - 2
                            s.wait("scalar", VE, ve_rope[(pidx // NT, pidx % NT)])

                        def dstf(plo, phi, qdst=qdst, cbase=cbase):
                            return qdst[plo:phi, cbase:cbase + T]
                        sc_qcopy[(m, tt)], ve_rope[(m, tt)] = rope_evac(
                            qp[b], dstf, coskb[:, lo:hi], srotb[:, lo:hi],
                            qscr[idx % 2], qrot[idx % 2])
                        sqb = (m * NT + tt) % 2
                        s.wait("scalar", PE, pe_qn_last[sqb])
                        s.emit("scalar", lambda e, b=b, sqb=sqb: e.activation(
                            sq_scr[sqb][:], qp[b][:], AF.Square), inc=SC,
                            chain=False)
                        sc_sq = SC.v
                        sc_qfree[(m, tt)] = sc_sq

                        def qn_mm(m=m, tt=tt, idx=idx, sqb=sqb, sc_sq=sc_sq):
                            nb = idx % 2
                            s.wait("tensor", SC, sc_sq)
                            if idx >= 2:
                                s.wait("tensor", VE, ve_qred[idx - 2])
                            s.emit("tensor", lambda e, nb=nb: e.matmul(
                                qn_pr[nb][:], ones_sb[:], sq_scr[sqb][:],
                                start=True, stop=True), inc=PE)
                            pe_qn_last[sqb] = PE.v
                            s.wait("vector", PE, PE.v)
                            ve_qred[idx] = s.emit(
                                "vector", lambda e, nb=nb, tt=tt: e.tensor_reduce(
                                    mqp[:, tt:tt + 1], qn_pr[nb][:], axis=AXX,
                                    op=ALU.max), inc=VE)
                            if tt == NT - 1:
                                ve_maxred[m] = s.emit(
                                    "vector", lambda e, m=m: e.tensor_reduce(
                                        maxq2[:, m:m + 1], mqp[:], axis=AXX,
                                        op=ALU.max), inc=VE)
                        pend.append(qn_mm)

                    for m in range(NCH):
                        dw_c = col_load(wq, m, m % 2)
                        for tt in range(NT):
                            q_tile(m, tt, dw_c)
                    flush_pend(0)

                    kv_state = {"pe_kn": 0, "pe_tp_last": {}, "ve_tp": {}}
                    ve_kv = {}
                    sc_kvcopy = {}

                    def kv_tile(mkv, tt, dw_need):
                        idx = mkv * NT + tt
                        b = (64 + idx) % 4
                        lo, hi = tt * T, (tt + 1) * T
                        flush_pend(1)
                        s.wait("tensor", DWp[dw_need[0]], dw_need[1])
                        if idx >= 4:
                            pidx = idx - 4
                            pk = (pidx // NT, pidx % NT)
                            if pk in sc_kvfree:
                                s.wait("tensor", SC, sc_kvfree[pk])
                            else:
                                s.wait("tensor", VE, ve_kv.get(pk))
                        else:
                            s.wait("tensor", SC, sc_qfree[(NCH - 1, idx)])
                        for k in range(NCH):
                            s.emit("tensor", lambda e, b=b, mkv=mkv, k=k, lo=lo, hi=hi: e.matmul(
                                qp[b][:], wcol_a[mkv % 2][:, k * 128:(k + 1) * 128],
                                xb[:, k * NTOK + lo: k * NTOK + hi],
                                start=(k == 0), stop=(k == NCH - 1)),
                                inc=(PE if k == NCH - 1 else None))
                        pe_col_last[mkv % 2] = PE.v
                        pe_kv = PE.v
                        if mkv == 0:
                            s.wait("scalar", PE, pe_kv)
                            if idx >= 2:
                                pk = (idx - 2 // NT, (idx - 2) % NT)
                                pk = ((idx - 2) // NT, (idx - 2) % NT)
                                if pk in ve_kv and pk in sc_kvcopy:
                                    s.wait("scalar", VE, ve_kv[pk])
                            else:
                                s.wait("scalar", VE, ve_rope[(NCH - 1, 2 + idx)])

                            def dstk(plo, phi, lo=lo, hi=hi):
                                return kT[plo:phi, lo:hi]
                            sc_kvcopy[(mkv, tt)], ve_kv[(mkv, tt)] = rope_evac(
                                qp[b], dstk, coskb[:, lo:hi], srotb[:, lo:hi],
                                qscr[idx % 2], qrot[idx % 2])
                            sqb = tt % 2
                            s.wait("scalar", PE, pe_qn_last[sqb])
                            s.emit("scalar", lambda e, b=b, sqb=sqb: e.activation(
                                sq_scr[sqb][:], qp[b][:], AF.Square), inc=SC,
                                chain=False)
                            sck = SC.v
                            sc_kvfree[(mkv, tt)] = sck

                            def kn_mm(tt=tt, sqb=sqb, sck=sck):
                                gidx = 64 + tt
                                nb = gidx % 2
                                s.wait("tensor", SC, sck)
                                s.wait("tensor", VE, ve_qred[gidx - 2])
                                s.emit("tensor", lambda e, nb=nb: e.matmul(
                                    qn_pr[nb][:], ones_sb[:], sq_scr[sqb][:],
                                    start=True, stop=True), inc=PE)
                                kv_state["pe_kn"] = PE.v
                                pe_qn_last[sqb] = PE.v
                                s.wait("vector", PE, PE.v)
                                ve_qred[gidx] = s.emit(
                                    "vector", lambda e, nb=nb, tt=tt: e.tensor_reduce(
                                        mqp[:, tt:tt + 1], qn_pr[nb][:], axis=AXX,
                                        op=ALU.max), inc=VE)
                            pend.append(kn_mm)
                        else:
                            s.wait("vector", PE, pe_kv)
                            if tt >= 2:
                                s.wait("vector", PE, kv_state["pe_tp_last"].get(tt - 2, 0))
                            ve_kv[(mkv, tt)] = s.emit("vector", lambda e, b=b, tt=tt: e.tensor_copy(
                                vtscr[tt % 2][:], qp[b][:]), inc=VE)
                            for j in range(4):
                                jj = tt * 4 + j
                                tb = jj % 2
                                s.wait("tensor", VE, ve_kv[(mkv, tt)])
                                s.wait("tensor", VE, kv_state["ve_tp"].get(jj - 2, 0))
                                s.emit("tensor", lambda e, tt=tt, j=j, tb=tb: e.transpose(
                                    tp_psb[tb][:],
                                    vtscr[tt % 2][:, j * 128:(j + 1) * 128],
                                    ident[:]), inc=PE)
                                pe_tp = PE.v
                                s.wait("vector", PE, pe_tp)
                                kv_state["ve_tp"][jj] = s.emit(
                                    "vector", lambda e, jj=jj, tb=tb: e.tensor_copy(
                                        v_sb[:, jj * 128:(jj + 1) * 128],
                                        tp_psb[tb][:]), inc=VE, chain=False)
                            kv_state["pe_tp_last"][tt] = pe_tp

                    for mkv in range(2):
                        dw_c = col_load(wkv, mkv, mkv % 2)
                        for tt in range(NT):
                            kv_tile(mkv, tt, dw_c)
                    flush_pend(0)

                    s.emit("vector", lambda e: e.tensor_reduce(
                        kmax2[:], mqp[:], axis=AXX, op=ALU.max), inc=VE)
                    ve_kmax = VE.v

                    s.wait("vector", VE, ve_kmax)
                    s.emit("vector", lambda e: e.tensor_scalar(
                        out=negM[:], in0=maxq2[:], scalar1=kmax2[:], scalar2=None,
                        op0=ALU.mult), inc=VE)
                    s.wait("scalar", VE, VE.v)
                    s.emit("scalar", lambda e: e.activation(negM[:], negM[:], AF.Sqrt), inc=SC)
                    s.wait("vector", SC, SC.v)
                    s.emit("vector", lambda e: e.tensor_scalar(
                        out=negM[:], in0=negM[:], scalar1=-SCALE, scalar2=None,
                        op0=ALU.mult), inc=VE)

                    s.flush(block)
                    if DEV_TAP_FN is not None:
                        DEV_TAP_FN("A", dict(
                            nc=nc, block=block, Sched=Sched, sems=dict(
                                PE=PE, SC=SC, VE=VE, GP=GP, DO=DO),
                            kT=kT, v_sb=v_sb, negM=negM, maxq2=maxq2, kmax2=kmax2,
                            q_h=q_h, xb=xb, cosk=cosk,
                            sink=sink, scale_sb=scale_sb))

            with ExitStack() as escd:
                def sbc(name, shape, dtype):
                    return escd.enter_context(nc.sbuf_tensor(name, shape, dtype))
                wcol_c = [sbc(f"wcol_c{i}", [128, D], BF16) for i in range(2)]
                xcol = sbc("xcol", [128, 1024], F32)
                cast_w = [sbc(f"cast_w{i}", [128, 2 * T], BF16) for i in range(2)]
                sq_w = [sbc(f"sq_w{i}", [128, 2 * T], BF16) for i in range(2)]
                mean_w = sbc("mean_w", [128, 2 * T], F32)
                rstd_w = sbc("rstd_w", [128, 2 * T], F32)
                wg1 = xcol
                xcol_b = [xcol, rstd_w]
                g_w = mean_w


                def prefetch_cols(sch, src_ap, tag, h, ms):
                    for m in ms:
                        buf = m % 2
                        sch.wait("gpsimd", PE, state["pe_col"][buf])
                        sch.emit("gpsimd", lambda e, m=m, buf=buf, src_ap=src_ap:
                                 e.dma_start(out=wcol_c[buf][:], in_=src_ap[m]),
                                 inc=DWp[buf], inc_by=16, is_dma=True)
                        state["wpref"][(tag, h, m)] = (buf, DWp[buf].v)

                def phase_B(s, h, o_ps, z_ps, s_ps, oTf):
                    def one_head(t, hh):
                        g = state["git"]
                        ob = 0
                        Kt = 4 * t + 4
                        P = Kt // 2
                        base = state["chunk"]
                        qcol0 = hh * 1024 + (t % 2) * T
                        qsrc = q_h[h]
                        pe_s = {}
                        sc_exp = {}
                        gp_mask = {}

                        def emit_S(kc):
                            lo = max(0, kc - 4 * t) * 128
                            p, hf = kc // 2, kc % 2
                            s.emit("tensor", lambda e, kc=kc, lo=lo, p=p, hf=hf:
                                   e.matmul(s_ps[p % 3][:, hf * T + lo: hf * T + T],
                                            kT[:, kc * 128:(kc + 1) * 128],
                                            qsrc[:, qcol0 + lo: qcol0 + T],
                                            start=True, stop=True), inc=PE)
                            pe_s[kc] = PE.v

                        def emit_AV(kc):
                            lo = max(0, kc - 4 * t) * 128
                            n = base + kc
                            jp, hf = (n // 2) % 2, kc % 2
                            if kc >= 4 * t:
                                s.wait("tensor", GP, gp_mask[kc])
                            else:
                                s.wait("tensor", SC, sc_exp[kc])
                            if kc == 0:
                                s.wait("tensor", VE, state["ve_norm"].get(g - 1, 0))
                            s.emit("tensor", lambda e, kc=kc, lo=lo, jp=jp, hf=hf, ob=ob, Kt=Kt:
                                   e.matmul(o_ps[ob][:, lo:T],
                                            v_sb[:, kc * 128:(kc + 1) * 128],
                                            pT[jp][:, hf * T + lo: hf * T + T],
                                            start=(kc == 0), stop=(kc == Kt - 1)), inc=PE)

                        def emit_Z(kc):
                            lo = max(0, kc - 4 * t) * 128
                            n = base + kc
                            jp, hf = (n // 2) % 2, kc % 2
                            s.emit("tensor", lambda e, kc=kc, lo=lo, jp=jp, hf=hf, ob=ob, Kt=Kt:
                                   e.matmul(z_ps[ob][:, lo:T], ones_sb[:],
                                            pT[jp][:, hf * T + lo: hf * T + T],
                                            start=(kc == 0), stop=(kc == Kt - 1)), inc=PE)
                            state["pe_z"][n] = PE.v

                        def emit_exp(p):
                            n0 = base + 2 * p
                            jp = (n0 // 2) % 2
                            s.wait("scalar", PE, pe_s[2 * p + 1])
                            s.wait("scalar", PE, state["pe_z"].get(n0 - 3, 0))
                            if 2 * p >= 4 * t:
                                for hf in range(2):
                                    kc = 2 * p + hf
                                    lo = max(0, kc - 4 * t) * 128
                                    s.emit("scalar", lambda e, jp=jp, hf=hf, lo=lo, p=p, hh=hh:
                                           e.activation(pT[jp][:, hf * T + lo: hf * T + T],
                                                        s_ps[p % 3][:, hf * T + lo: hf * T + T],
                                                        AF.Exp, bias=negM[:, hh:hh + 1],
                                                        scale=SCALE), inc=SC, chain=False)
                                    sc_exp[kc] = SC.v
                            else:
                                s.emit("scalar", lambda e, jp=jp, p=p, hh=hh:
                                       e.activation(pT[jp][:, :],
                                                    s_ps[p % 3][:, :],
                                                    AF.Exp, bias=negM[:, hh:hh + 1],
                                                    scale=SCALE), inc=SC, chain=False)
                                sc_exp[2 * p] = sc_exp[2 * p + 1] = SC.v
                            for hf in range(2):
                                kc = 2 * p + hf
                                if kc >= 4 * t:
                                    lo = (kc - 4 * t) * 128
                                    s.wait("gpsimd", SC, sc_exp[kc])
                                    s.emit("gpsimd", lambda e, jp=jp, hf=hf, lo=lo:
                                           e.affine_select(
                                               pT[jp][:, hf * T + lo: hf * T + lo + 128],
                                               pT[jp][:, hf * T + lo: hf * T + lo + 128],
                                               [[1, 128]], ALU.is_ge, 0.0,
                                               base=0, channel_multiplier=-1), inc=GP,
                                           chain=False)
                                    gp_mask[kc] = GP.v

                        def emit_AVZ_pair(p):
                            emit_AV(2 * p)
                            emit_AV(2 * p + 1)
                            emit_Z(2 * p)
                            emit_Z(2 * p + 1)

                        for p in range(P):
                            emit_S(2 * p)
                            emit_S(2 * p + 1)
                            if p >= 2:
                                emit_AVZ_pair(p - 2)
                            if p >= 1:
                                emit_exp(p - 1)
                        emit_exp(P - 1)
                        emit_AVZ_pair(P - 2)
                        emit_AVZ_pair(P - 1)
                        state["chunk"] += Kt

                        ocol = hh * 1024 + (t % 2) * T
                        s.wait("scalar", PE, state["pe_z"][base + Kt - 1])
                        s.emit("scalar", lambda e, ob=ob: e.activation(
                            rz[ob][:], z_ps[ob][:], AF.Ln), chain=False)
                        sc_rz = s.emit("scalar", lambda e, ob=ob: e.activation(
                            rz[ob][:], rz[ob][:], AF.Exp, scale=-1.0), inc=SC)
                        s.wait("vector", SC, sc_rz)
                        state["ve_norm"][g] = s.emit(
                            "vector", lambda e, ob=ob, ocol=ocol: e.tensor_mul(
                                oTf[:, ocol:ocol + T], o_ps[ob][:], rz[ob][:]), inc=VE,
                            chain=False)
                        state["git"] += 1
                        if state["dapply"]:
                            state["dapply"].pop(0)(s)

                    for t in (2 * h, 2 * h + 1):
                        for hh in range(HEADS):
                            one_head(t, hh)
                    while state["dapply"]:
                        state["dapply"].pop(0)(s)

                def phase_C(s, h, y_ps, mn_ps, sq_ps, oTf, s_h, x1b_h):
                    pe_y = {}
                    ve_evac = {}
                    pe_stats = {}

                    def one_m(m):
                        buf = m % 2
                        if ("C", h, m) in state["wpref"]:
                            dw_c = state["wpref"].pop(("C", h, m))
                        else:
                            s.wait("gpsimd", PE, state["pe_col"][buf])
                            s.emit("gpsimd", lambda e, m=m, buf=buf: e.dma_start(
                                out=wcol_c[buf][:], in_=wout[m]), inc=DWp[buf], inc_by=16,
                                is_dma=True)
                            dw_c = (buf, DWp[buf].v)
                        xb2 = m % 2
                        if m == 0 and h in state["xpref"]:
                            dx_x = state["xpref"].pop(h)
                        else:
                            if m > 1:
                                s.wait("sync", VE, ve_evac[(m - 2, 1)])
                            s.emit("sync", lambda e, m=m, h=h, xb2=xb2: e.dma_start(
                                out=xcol_b[xb2][:],
                                in_=xT[m * 128:(m + 1) * 128, h * 1024:(h + 1) * 1024]),
                                inc=DXCp[xb2], inc_by=16)
                            dx_x = DXCp[xb2].v
                        for tt in range(2):
                            b2 = (2 * m + tt) % 2
                            idx = 2 * m + tt
                            flush_pend(1)
                            s.wait("tensor", DWp[dw_c[0]], dw_c[1])
                            if idx >= 2:
                                s.wait("tensor", VE, ve_evac[((idx - 2) // 2, (idx - 2) % 2)])
                            for k in range(NCH):
                                s.emit("tensor", lambda e, b2=b2, m=m, k=k, tt=tt: e.matmul(
                                    y_ps[b2][:], wcol_c[m % 2][:, k * 128:(k + 1) * 128],
                                    oTf[:, k * 1024 + tt * T: k * 1024 + (tt + 1) * T],
                                    start=(k == 0), stop=(k == NCH - 1)),
                                    inc=(PE if k == NCH - 1 else None))
                            pe_y[(m, tt)] = PE.v
                            state["pe_col"][m % 2] = PE.v
                            scol = m * 1024 + tt * T
                            s.wait("vector", PE, pe_y[(m, tt)])
                            s.wait("vector", DXCp[xb2], dx_x)
                            ve_evac[(m, tt)] = s.emit(
                                "vector", lambda e, b2=b2, scol=scol, tt=tt, xb2=xb2:
                                e.tensor_add(
                                    s_h[:, scol:scol + T], y_ps[b2][:],
                                    xcol_b[xb2][:, tt * T:(tt + 1) * T]), inc=VE,
                                chain=False)
                            s.wait("scalar", VE, ve_evac[(m, tt)])
                            if idx >= 4:
                                s.wait("scalar", PE, pe_stats[((idx - 4) // 2, (idx - 4) % 2)])
                            wb, wt = (idx // 2) % 2, idx % 2
                            s.emit("scalar", lambda e, wb=wb, wt=wt, scol=scol: e.activation(
                                cast_w[wb][:, wt * T:(wt + 1) * T],
                                s_h[:, scol:scol + T], AF.Copy), chain=False)
                            s.emit("scalar", lambda e, wb=wb, wt=wt, scol=scol: e.activation(
                                sq_w[wb][:, wt * T:(wt + 1) * T],
                                s_h[:, scol:scol + T], AF.Square), inc=SC, chain=False)
                            sc_c = SC.v

                            def st_mm(m=m, tt=tt, wb=wb, wt=wt, sc_c=sc_c):
                                s.wait("tensor", SC, sc_c)
                                s.emit("tensor", lambda e, tt=tt: e.matmul(
                                    mn_ps[:, tt * T:(tt + 1) * T], ones_sb[:],
                                    cast_w[wb][:, wt * T:(wt + 1) * T],
                                    start=(m == 0), stop=(m == NCH - 1)), inc=PE)
                                s.emit("tensor", lambda e, tt=tt: e.matmul(
                                    sq_ps[:, tt * T:(tt + 1) * T], ones_sb[:],
                                    sq_w[wb][:, wt * T:(wt + 1) * T],
                                    start=(m == 0), stop=(m == NCH - 1)), inc=PE)
                                pe_stats[(m, tt)] = PE.v
                            pend.append(st_mm)

                    pend = []

                    def flush_pend(n_keep=0):
                        while len(pend) > n_keep:
                            pend.pop(0)()

                    for m in range(NCH):
                        one_m(m)
                    flush_pend(0)
                    prefetch_cols(s, wffn, "D", h, [0, 1])

                    if DEV_C_STOP == "mm":
                        return
                    ve_rstd = ln_stats(s, mn_ps[:], sq_ps[:], mean_w[:],
                                       rstd_w[:], pe_stats[(NCH - 1, 1)], wg1)
                    if DEV_C_STOP == "stats":
                        return

                    sc_x1b = {}
                    for tt in range(2):
                        for m in range(NCH):
                            scol = m * 1024 + tt * T
                            s.wait("vector", ve_rstd[0], ve_rstd[1])
                            s.emit("vector", lambda e, scol=scol, tt=tt: e.tensor_sub(
                                s_h[:, scol:scol + T], s_h[:, scol:scol + T],
                                mean_w[:, tt * T:(tt + 1) * T]), chain=False)
                            vex = s.emit("vector", lambda e, scol=scol, tt=tt: e.tensor_mul(
                                s_h[:, scol:scol + T], s_h[:, scol:scol + T],
                                rstd_w[:, tt * T:(tt + 1) * T]), inc=VE)
                            s.wait("scalar", VE, vex)
                            s.emit("scalar", lambda e, scol=scol: e.activation(
                                x1b_h[:, scol:scol + T], s_h[:, scol:scol + T],
                                AF.Copy), inc=SC, chain=False)
                        sc_x1b[tt] = SC.v
                    state["sc_x1b"] = sc_x1b
                    state["c_ve"] = VE.v
                    state["c_gp"] = GP.v

                def ln_stats(s, m_ps, q_ps, m_sb, r_sb, pe_done, scr):
                    s.wait("vector", PE, pe_done)
                    s.emit("vector", lambda e: e.tensor_scalar(
                        out=m_sb, in0=m_ps, scalar1=1.0 / D, scalar2=None,
                        op0=ALU.mult))
                    s.emit("vector", lambda e: e.scalar_tensor_tensor(
                        out=r_sb, in0=m_ps, scalar=1.0 / D, in1=m_sb,
                        op0=ALU.mult, op1=ALU.mult))
                    s.emit("vector", lambda e: e.scalar_tensor_tensor(
                        out=scr[:], in0=q_ps, scalar=1.0 / D, in1=r_sb,
                        op0=ALU.mult, op1=ALU.subtract), inc=VE)
                    s.wait("scalar", VE, VE.v)
                    s.emit("scalar", lambda e: e.activation(
                        r_sb, scr[:], AF.Ln, bias=eps_sb[:, 0:1]))
                    scv = s.emit("scalar", lambda e: e.activation(
                        r_sb, r_sb, AF.Exp, scale=-0.5), inc=SC)
                    return (SC, scv)

                def phase_D(s, h, f_ps, fsq_ps, mn_ps, sq_ps, f_h, s_h, x1b_h,
                            defer_apply=False):
                    pe_f = {}
                    sc_fe = {}
                    pe_fsq = {}

                    def one_m(m):
                        buf = m % 2
                        if ("D", h, m) in state["wpref"]:
                            dw_c = state["wpref"].pop(("D", h, m))
                        else:
                            s.wait("gpsimd", PE, state["pe_col"][buf])
                            s.emit("gpsimd", lambda e, m=m, buf=buf: e.dma_start(
                                out=wcol_c[buf][:], in_=wffn[m]), inc=DWp[buf], inc_by=16,
                                is_dma=True)
                            dw_c = (buf, DWp[buf].v)
                        for tt in range(2):
                            b2 = (2 * m + tt) % 2
                            idx = 2 * m + tt
                            flush_pend(1)
                            s.wait("tensor", DWp[dw_c[0]], dw_c[1])
                            s.wait("tensor", SC, state["sc_x1b"][tt])
                            if idx >= 2:
                                s.wait("tensor", SC, sc_fe[idx - 2])
                            for k in range(NCH):
                                s.emit("tensor", lambda e, b2=b2, m=m, k=k, tt=tt: e.matmul(
                                    f_ps[b2][:], wcol_c[m % 2][:, k * 128:(k + 1) * 128],
                                    x1b_h[:, k * 1024 + tt * T: k * 1024 + (tt + 1) * T],
                                    start=(k == 0), stop=(k == NCH - 1)),
                                    inc=(PE if k == NCH - 1 else None))
                            pe_f[idx] = PE.v
                            state["pe_col"][m % 2] = PE.v
                            fcol = m * 1024 + tt * T
                            s.wait("scalar", PE, pe_f[idx])
                            if idx >= 4:
                                s.wait("scalar", PE, pe_fsq[idx - 4])
                            wb, wt = (idx // 2) % 2, idx % 2
                            s.emit("scalar", lambda e, b2=b2, fcol=fcol, m=m: e.activation(
                                f_h[:, fcol:fcol + T], f_ps[b2][:], AF.Copy,
                                scale=scale_sb[:, m:m + 1]), chain=False)
                            s.emit("scalar", lambda e, wb=wb, wt=wt, fcol=fcol: e.activation(
                                sq_w[wb][:, wt * T:(wt + 1) * T],
                                f_h[:, fcol:fcol + T], AF.Square), inc=SC)
                            sc_fe[idx] = SC.v

                            def fsq_mm(idx=idx, m=m, tt=tt, wb=wb, wt=wt, sc=sc_fe[idx]):
                                s.wait("tensor", SC, sc)
                                s.emit("tensor", lambda e, tt=tt: e.matmul(
                                    fsq_ps[:, tt * T:(tt + 1) * T], ones_sb[:],
                                    sq_w[wb][:, wt * T:(wt + 1) * T],
                                    start=(m == 0), stop=(m == NCH - 1)), inc=PE)
                                pe_fsq[idx] = PE.v
                            pend.append(fsq_mm)

                    pend = []

                    def flush_pend(n_keep=0):
                        while len(pend) > n_keep:
                            pend.pop(0)()

                    for m in range(NCH):
                        one_m(m)
                    flush_pend(0)

                    s.wait("scalar", PE, pe_fsq[2 * NCH - 1])
                    s.wait("scalar", GP, state["c_gp"])
                    s.wait("vector", GP, state["c_gp"])
                    s.emit("scalar", lambda e: e.activation(
                        wg1[:], fsq_ps[:], AF.Sqrt), inc=SC)
                    s.wait("vector", SC, SC.v)
                    s.emit("vector", lambda e: e.tensor_scalar(
                        out=wg1[:], in0=wg1[:], scalar1=1e-7, scalar2=None,
                        op0=ALU.max), inc=VE)
                    s.wait("scalar", VE, VE.v)
                    s.emit("scalar", lambda e: e.activation(
                        rstd_w[:], wg1[:], AF.Tanh), inc=SC)
                    s.emit("vector", lambda e: e.tensor_scalar(
                        out=g_w[:], in0=wg1[:], scalar1=1e-7, scalar2=None,
                        op0=ALU.add), inc=VE)
                    s.wait("scalar", VE, VE.v)
                    s.emit("scalar", lambda e: e.activation(g_w[:], g_w[:], AF.Ln))
                    s.emit("scalar", lambda e: e.activation(
                        g_w[:], g_w[:], AF.Exp, scale=-1.0), inc=SC)
                    s.wait("vector", SC, SC.v)
                    s.emit("vector", lambda e: e.tensor_mul(g_w[:], g_w[:], rstd_w[:]))
                    s.emit("vector", lambda e: e.tensor_mul(rstd_w[:], wg1[:], g_w[:]))
                    s.emit("vector", lambda e: e.tensor_scalar(
                        out=rstd_w[:], in0=rstd_w[:], scalar1=1e-7, scalar2=None,
                        op0=ALU.max), inc=VE)
                    s.wait("scalar", VE, VE.v)
                    s.emit("scalar", lambda e: e.activation(rstd_w[:], rstd_w[:], AF.Ln))
                    s.emit("scalar", lambda e: e.activation(
                        rstd_w[:], rstd_w[:], AF.Exp, scale=-1.0), inc=SC)
                    s.wait("vector", SC, SC.v)
                    s.emit("vector", lambda e: e.tensor_scalar(
                        out=rstd_w[:], in0=rstd_w[:], scalar1=0.99, scalar2=1.0,
                        op0=ALU.mult, op1=ALU.min))
                    ve_gate = s.emit("vector", lambda e: e.tensor_mul(
                        g_w[:], g_w[:], rstd_w[:]), inc=VE)

                    pe_stats2 = {}
                    for m in range(NCH):
                        b2 = m % 2
                        wlo = m * 1024
                        s.emit("vector", lambda e, wlo=wlo: e.tensor_mul(
                            wg1[:], f_h[:, wlo:wlo + 1024], g_w[:]))
                        ve_s2 = s.emit("vector", lambda e, wlo=wlo: e.tensor_add(
                            s_h[:, wlo:wlo + 1024], s_h[:, wlo:wlo + 1024],
                            wg1[:]), inc=VE)
                        s.wait("scalar", VE, ve_s2)
                        if m >= 2:
                            s.wait("scalar", PE, pe_stats2[m - 2])
                        s.emit("scalar", lambda e, b2=b2, wlo=wlo: e.activation(
                            cast_w[b2][:], s_h[:, wlo:wlo + 1024], AF.Copy), chain=False)
                        sc_cw = s.emit("scalar", lambda e, b2=b2, wlo=wlo: e.activation(
                            sq_w[b2][:], s_h[:, wlo:wlo + 1024], AF.Square), inc=SC,
                            chain=False)
                        s.wait("tensor", SC, sc_cw)
                        for tt in range(2):
                            s.emit("tensor", lambda e, b2=b2, tt=tt, m=m: e.matmul(
                                mn_ps[:, tt * T:(tt + 1) * T], ones_sb[:],
                                cast_w[b2][:, tt * T:(tt + 1) * T],
                                start=(m == 0), stop=(m == NCH - 1)))
                            s.emit("tensor", lambda e, b2=b2, tt=tt, m=m: e.matmul(
                                sq_ps[:, tt * T:(tt + 1) * T], ones_sb[:],
                                sq_w[b2][:, tt * T:(tt + 1) * T],
                                start=(m == 0), stop=(m == NCH - 1)),
                                inc=(PE if tt == 1 else None))
                        pe_stats2[m] = PE.v

                    ve_rstd2 = ln_stats(s, mn_ps[:], sq_ps[:], mean_w[:],
                                        rstd_w[:], pe_stats2[NCH - 1], wg1)

                    def emit_apply(sb, m):
                        wlo = m * 1024
                        sb.wait("vector", ve_rstd2[0], ve_rstd2[1])
                        sb.emit("vector", lambda e, wlo=wlo: e.tensor_sub(
                            s_h[:, wlo:wlo + 1024], s_h[:, wlo:wlo + 1024],
                            mean_w[:]), chain=False)
                        vex = sb.emit("vector", lambda e, wlo=wlo: e.tensor_mul(
                            s_h[:, wlo:wlo + 1024], s_h[:, wlo:wlo + 1024],
                            rstd_w[:]), inc=VE)
                        sb.wait("sync", VE, vex)
                        sb.emit("sync", lambda e, m=m, h=h, wlo=wlo: e.dma_start(
                            out=out[m * 128:(m + 1) * 128,
                                    h * 1024:(h + 1) * 1024],
                            in_=s_h[:, wlo:wlo + 1024]), inc=DO, inc_by=16)

                    if defer_apply:
                        for m in range(NCH):
                            state["dapply"].append(
                                lambda sb, m=m: emit_apply(sb, m))
                    else:
                        for m in range(NCH):
                            emit_apply(s, m)

                ess = [ExitStack(), ExitStack()]
                for h in range(2):
                    with ExitStack() as esh:
                        oTf = esh.enter_context(
                            nc.sbuf_tensor(f"oTf{h}", [128, NCH * 1024], BF16))
                        f_h = oTf

                        if f"B{h}" in phases:
                            s = Sched(SERIAL)
                            s.barrier_in(("tensor", "scalar", "vector", "gpsimd", "sync"), BSEMS_B)
                            with nc.psum_tensor(f"o_ps{h}a", [128, T], F32) as opa, \
                                 nc.psum_tensor(f"z_ps{h}a", [128, T], F32) as zpa, \
                                 nc.psum_tensor(f"s_ps{h}a", [128, 2 * T], F32) as spa, \
                                 nc.psum_tensor(f"s_ps{h}b", [128, 2 * T], F32) as spb, \
                                 nc.psum_tensor(f"s_ps{h}c", [128, 2 * T], F32) as spc:
                                phase_B(s, h, [opa], [zpa], [spa, spb, spc], oTf)
                                prefetch_cols(s, wout, "C", h, [0, 1])
                                s.emit("sync", lambda e, h=h: e.dma_start(
                                    out=xcol[:],
                                    in_=xT[0:128, h * 1024:(h + 1) * 1024]),
                                    inc=DXC, inc_by=16)
                                state["xpref"][h] = DXC.v
                                s.flush(block)
                            if DEV_TAP_FN is not None:
                                DEV_TAP_FN(f"B{h}", dict(
                                    nc=nc, block=block, Sched=Sched, sems=dict(
                                        PE=PE, SC=SC, VE=VE, GP=GP, DO=DO),
                                    oTf=oTf))

                        if h == 1:
                            ess[0].close()
                        esq[h].close()
                        s_h = ess[h].enter_context(
                            nc.sbuf_tensor(f"s_h{h}", [128, NCH * 1024], F32,
                                           side="right"))
                        x1b_h = esh.enter_context(
                            nc.sbuf_tensor(f"x1b{h}", [128, NCH * 1024], BF16))

                        if f"C{h}" in phases:
                            s = Sched(SERIAL)
                            s.barrier_in(("tensor", "scalar", "vector", "gpsimd", "sync"), BSEMS)
                            with nc.psum_tensor(f"y_ps{h}a", [128, T], F32) as ypa, \
                                 nc.psum_tensor(f"y_ps{h}b", [128, T], F32) as ypb, \
                                 nc.psum_tensor(f"mn_ps{h}", [128, 2 * T], F32) as mpw, \
                                 nc.psum_tensor(f"sx_ps{h}", [128, 2 * T], F32) as sxw:
                                phase_C(s, h, [ypa, ypb], mpw, sxw,
                                        oTf, s_h, x1b_h)
                                s.flush(block)
                            if DEV_TAP_FN is not None:
                                DEV_TAP_FN(f"C{h}", dict(
                                    nc=nc, block=block, Sched=Sched, sems=dict(
                                        PE=PE, SC=SC, VE=VE, GP=GP, DO=DO),
                                    oTf=oTf, s_h=s_h, x1b_h=x1b_h))

                        if f"D{h}" in phases:
                            s = Sched(SERIAL)
                            with nc.psum_tensor(f"f_ps{h}a", [128, T], F32) as fpa, \
                                 nc.psum_tensor(f"f_ps{h}b", [128, T], F32) as fpb, \
                                 nc.psum_tensor(f"fq_ps{h}", [128, 2 * T], F32) as fqw, \
                                 nc.psum_tensor(f"m2_ps{h}", [128, 2 * T], F32) as m2w, \
                                 nc.psum_tensor(f"q2_ps{h}", [128, 2 * T], F32) as q2w:
                                phase_D(s, h, [fpa, fpb], fqw,
                                        m2w, q2w, f_h, s_h, x1b_h,
                                        defer_apply=(h == 0 and "B1" in phases))
                                s.flush(block)

                ess[1].close()
                s = Sched(SERIAL)
                s.wait("sync", DO, DO.v)
                s.flush(block)

    return nc


def _host_tables():
    try:
        import jax
        import jax.numpy as jnp
        cpu = jax.devices("cpu")[0]
        with jax.default_device(cpu):
            e = jnp.arange(0, 128, 2, dtype=jnp.float32) / 128.0
            inv = 1.0 / 10000.0 ** e
            freqs = jnp.arange(NTOK, dtype=jnp.float32)[:, None] * inv[None, :]
            pos = jnp.concatenate((freqs, freqs), axis=-1)
            cos = np.asarray(jnp.cos(pos))
            sin = np.asarray(jnp.sin(pos))
    except Exception:
        inv = (1.0 / 10000.0 ** (np.arange(0, 128, 2, dtype=np.float64) / 128.0)).astype(np.float32)
        f = (np.arange(NTOK, dtype=np.float32)[:, None] * inv[None, :]).astype(np.float32)
        pos = np.concatenate([f, f], axis=-1)
        cos = np.cos(pos).astype(np.float32)
        sin = np.sin(pos).astype(np.float32)
    return np.ascontiguousarray(cos.T), np.ascontiguousarray(sin.T)


def _re_w(W):
    W = np.asarray(W, np.float32)
    ncols = W.shape[1] // 128
    return np.ascontiguousarray(
        W.reshape(16, 128, ncols, 128).transpose(2, 1, 0, 3)
    ).reshape(ncols, 128, 2048)


def kernel(L0_attn_x, L0_attn_Wq, L0_attn_Wkv, L0_attn_Wout, L0_ffn_weight, L0_ffn_scale):
    x = np.asarray(L0_attn_x, np.float32)
    B = x.shape[0]
    WQ = _re_w(L0_attn_Wq)
    WKV = _re_w(L0_attn_Wkv)
    WOUT = _re_w(L0_attn_Wout)
    WFFN = _re_w(L0_ffn_weight)
    scale_t = np.ascontiguousarray(np.asarray(L0_ffn_scale, np.float32).reshape(16, 128).T)
    cosk_t, sink_t = _host_tables()

    nc = build_nc()
    in_maps = []
    for b in range(B):
        in_maps.append({
            "xT": np.ascontiguousarray(x[b].T),
            "wq": WQ, "wkv": WKV, "wout": WOUT, "wffn": WFFN,
            "scale_t": scale_t, "cosk_t": cosk_t, "sink_t": sink_t,
        })
    res = run_bass_kernel_spmd(nc, in_maps, core_ids=list(range(B)))
    return np.stack([np.ascontiguousarray(np.asarray(res.results[b]["out"]).T)
                     for b in range(B)])
